# version 01
# baseline (speedup 1.0000x reference)
import numpy as np
from contextlib import ExitStack

import concourse.bass as bass
import concourse.mybir as mybir
from concourse.bass_utils import run_bass_kernel_spmd
from concourse.alu_op_type import AluOpType as Op

F32 = mybir.dt.float32
I32 = mybir.dt.int32
BF16 = mybir.dt.bfloat16
AF = mybir.ActivationFunctionType
AX = mybir.AxisListType.X

T = 2048
D = 1024
H = 16
CH = 64
NST = 64
I_FF = 4096
NT = T // 128
NC = D // 128
TSUP = 512
NSUP = T // TSUP

EXP_BITS = 0x7F800000
C_REC8 = float(0x80800000)
C_PM3 = float(-(3 << 23))


def _sem_counter():
    return {"v": 0}


class Prog:

    def __init__(self, nc):
        self.nc = nc
        self.q = {"sync": [], "scalar": [], "vector": [], "tensor": [], "gpsimd": []}
        self.cnt = {}

    def on(self, eng, fn):
        self.q[eng].append(fn)

    def inc(self, sem_name, by):
        self.cnt[sem_name] = self.cnt.get(sem_name, 0) + by
        return self.cnt[sem_name]

    def val(self, sem_name):
        return self.cnt.get(sem_name, 0)


def build_layouts(inputs):
    W_XBC = np.asarray(inputs["L0_attn_W_XBC"])
    b_XBC = np.asarray(inputs["L0_attn_b_XBC"])
    W_gate = np.asarray(inputs["L0_attn_W_gate"])
    b_gate = np.asarray(inputs["L0_attn_b_gate"])
    W_logA = np.asarray(inputs["L0_attn_W_logA"])
    b_logA = np.asarray(inputs["L0_attn_b_logA"])
    W_out = np.asarray(inputs["L0_attn_W_out"])
    b_out = np.asarray(inputs["L0_attn_b_out"])

    wxbcT = np.zeros((128, NC * 192), np.float32)
    wgT = np.zeros((128, NC * 64), np.float32)
    wlogAT = np.zeros((128, NC * 16), np.float32)
    bBs = np.zeros((128, 16), np.float32)
    bCs = np.zeros((128, 16), np.float32)
    bXs = np.zeros((128, 16), np.float32)
    for h in range(H):
        c = h // 2
        po = 64 * (h % 2)
        wxbcT[po:po + 64, c * 192:(c + 1) * 192] = W_XBC[h].T
        wgT[po:po + 64, c * 64:(c + 1) * 64] = W_gate[h].T
        bXs[0:64, h] = b_XBC[h, 0:64]
        bBs[0:64, h] = b_XBC[h, 64:128]
        bCs[0:64, h] = b_XBC[h, 128:192]
    for h in range(H):
        for c in range(NC):
            rows = slice(c * 128, (c + 1) * 128)
            cols = slice(c * 16, (c + 1) * 16)
    wlogA_full = np.zeros((D, H), np.float32)
    for h in range(H):
        wlogA_full[64 * h:64 * (h + 1), h] = W_logA[h, 0, :]
    for c in range(NC):
        wlogAT[:, c * 16:(c + 1) * 16] = wlogA_full[c * 128:(c + 1) * 128, :]

    bgrep = np.tile(b_gate.reshape(1, H * 64), (128, 1)).astype(np.float32)
    woT = np.zeros((128, NC * D), np.float32)
    for c in range(NC):
        woT[:, c * D:(c + 1) * D] = W_out.T[c * 128:(c + 1) * 128, :]
    borep = np.tile(b_out.reshape(1, D), (128, 1)).astype(np.float32)

    ident = np.eye(128, dtype=np.float32)
    sel16 = np.repeat(np.eye(16, dtype=np.float32), 128, axis=1)
    import ml_dtypes as _ml
    z_16x128 = np.zeros((16, 128), np.float32)
    z_64f = np.zeros((64, 1024), np.float32)
    z_64b = np.zeros((64, 1024), np.float32).astype(_ml.bfloat16)
    z_stmp = np.zeros((1, 32), np.float32)
    epsrep = np.full((128, 1), 1e-5, np.float32)
    tri = np.triu(np.ones((128, 128), np.float32))
    ones128 = np.ones((128, 128), np.float32)
    blogA = b_logA.astype(np.float32).reshape(16, 1)

    return dict(wxbcT=wxbcT, wgT=wgT, wlogAT=wlogAT, bBs=bBs, bCs=bCs, bXs=bXs,
                bgrep=bgrep, woT=woT, borep=borep, ident=ident, tri=tri,
                ones128=ones128, blogA=blogA, sel16=sel16,
                z_16x128=z_16x128, z_64f=z_64f, z_64b=z_64b, z_stmp=z_stmp,
                epsrep=epsrep)


def emit_quant(P, name, src_ap, FD, tiles, gs6_pp, gs2_pp, vq_ap, sem, wait_list=None):
    nb = FD // 16
    amax = tiles["amax"]; scb = tiles["scb"]; ebi = tiles["ebi"]; ebf = tiles["ebf"]
    r8i = tiles["r8i"]; m8 = tiles["m8"]; n8a = tiles["n8a"]; n8b = tiles["n8b"]
    p8i = tiles["p8i"]; sc8 = tiles["sc8"]; rcp = tiles["rcp"]
    rcp2 = tiles["rcp2"]; sc8h = tiles["sc8h"]
    tm = tiles["tm"]; at = tiles["at"]; qa = tiles["qa"]; qb = tiles["qb"]
    ga = tiles["ga"]; gb = tiles["gb"]; sgn = tiles["sgn"]

    qv = "qv_" + name
    qs = "qs_" + name
    qg = "qg_" + name

    def V(fn):
        P.on("vector", fn)

    def S(fn):
        P.on("scalar", fn)

    def G(fn):
        P.on("gpsimd", fn)

    sem_h = sem

    def v_part1(e, src_ap=src_ap):
        e.tensor_reduce(out=amax[:, 0:nb], in_=src_ap.rearrange("p (a b) -> p a b", b=16),
                        axis=AX, op=Op.max, apply_absolute_value=True)
        e.tensor_scalar(out=scb[:, 0:nb], in0=amax[:, 0:nb], scalar1=gs6_pp,
                        scalar2=448.0, op0=Op.mult, op1=Op.min)
        e.tensor_scalar(out=ebi[:, 0:nb], in0=scb[:, 0:nb].bitcast(I32),
                        scalar1=EXP_BITS, scalar2=None, op0=Op.bitwise_and)
        e.tensor_scalar(out=ebf[:, 0:nb], in0=ebi[:, 0:nb].bitcast(F32),
                        scalar1=float(2.0 ** -6), scalar2=None, op0=Op.max)
        e.tensor_scalar(out=r8i[:, 0:nb], in0=ebf[:, 0:nb].bitcast(I32),
                        scalar1=-1.0, scalar2=C_REC8, op0=Op.mult, op1=Op.add)
        e.tensor_scalar(out=p8i[:, 0:nb], in0=ebf[:, 0:nb].bitcast(I32),
                        scalar1=C_PM3, scalar2=None, op0=Op.add)
        e.tensor_tensor(out=m8[:, 0:nb], in0=scb[:, 0:nb], in1=r8i[:, 0:nb].bitcast(F32),
                        op=Op.mult)
        e.tensor_scalar(out=n8a[:, 0:nb], in0=m8[:, 0:nb], scalar1=8.5, scalar2=None,
                        op0=Op.is_ge)
        for k in range(1, 8):
            s_, d_ = (n8a, n8b) if k % 2 == 1 else (n8b, n8a)
            e.scalar_tensor_tensor(out=d_[:, 0:nb], in0=m8[:, 0:nb], scalar=8.5 + k,
                                   in1=s_[:, 0:nb], op0=Op.is_ge, op1=Op.add)
        e.scalar_tensor_tensor(out=sc8[:, 0:nb], in0=n8b[:, 0:nb], scalar=8.0,
                               in1=p8i[:, 0:nb].bitcast(F32), op0=Op.add, op1=Op.mult)
        e.reciprocal(out=rcp[:, 0:nb], in_=sc8[:, 0:nb]).then_inc(sem_h["rb"], 1)
        e.tensor_scalar(out=sc8h[:, 0:nb], in0=sc8[:, 0:nb], scalar1=0.5,
                        scalar2=None, op0=Op.mult)
        e.wait_ge(sem_h["rb"], nrb)
        e.tensor_scalar(out=rcp2[:, 0:nb], in0=rcp[:, 0:nb], scalar1=gs2_pp,
                        scalar2=None, op0=Op.mult)
        e.tensor_tensor(out=tm[:, 0:FD].rearrange("p (a b) -> p a b", b=16),
                        in0=src_ap.rearrange("p (a b) -> p a b", b=16),
                        in1=rcp2[:, 0:nb].unsqueeze(-1).broadcast_to([128, nb, 16]),
                        op=Op.mult).then_inc(sem_h[qv], 1)
    V(v_part1)
    n_qv1 = P.inc(qv, 1)

    def s_part(e, src_ap=src_ap, n=n_qv1):
        e.wait_ge(sem_h[qv], n)
        e.activation(out=at[:, 0:FD], in_=tm[:, 0:FD], func=AF.Abs)
        e.activation(out=sgn[:, 0:FD], in_=src_ap, func=AF.Sign).then_inc(sem_h[qs], 1)
    S(s_part)
    n_qs = P.inc(qs, 1)

    def g_part(e, n=n_qs):
        e.wait_ge(sem_h[qs], n)
        e.tensor_scalar(out=ga[:, 0:FD], in0=at[:, 0:FD], scalar1=10.0, scalar2=2.0,
                        op0=Op.is_ge, op1=Op.mult)
        e.scalar_tensor_tensor(out=gb[:, 0:FD], in0=at[:, 0:FD], scalar=7.0,
                               in1=ga[:, 0:FD], op0=Op.is_ge, op1=Op.add)
        e.scalar_tensor_tensor(out=ga[:, 0:FD], in0=at[:, 0:FD], scalar=5.0,
                               in1=gb[:, 0:FD], op0=Op.is_ge, op1=Op.add).then_inc(sem_h[qg], 1)
    G(g_part)
    n_qg = P.inc(qg, 1)

    def v_part2(e, ns=n_qs, ng=n_qg):
        e.wait_ge(sem_h[qs], ns)
        e.tensor_scalar(out=qa[:, 0:FD], in0=at[:, 0:FD], scalar1=0.5, scalar2=None,
                        op0=Op.is_ge)
        for i, thr in enumerate((1.5, 2.5, 3.5)):
            s_, d_ = (qa, qb) if i % 2 == 0 else (qb, qa)
            e.scalar_tensor_tensor(out=d_[:, 0:FD], in0=at[:, 0:FD], scalar=thr,
                                   in1=s_[:, 0:FD], op0=Op.is_ge, op1=Op.add)
        e.wait_ge(sem_h[qg], ng)
        e.scalar_tensor_tensor(out=qa[:, 0:FD], in0=ga[:, 0:FD], scalar=2.0,
                               in1=qb[:, 0:FD], op0=Op.mult, op1=Op.add)
        e.tensor_tensor(out=qb[:, 0:FD], in0=qa[:, 0:FD], in1=sgn[:, 0:FD], op=Op.mult)
        e.tensor_tensor(out=vq_ap.rearrange("p (a b) -> p a b", b=16),
                        in0=qb[:, 0:FD].rearrange("p (a b) -> p a b", b=16),
                        in1=sc8h[:, 0:nb].unsqueeze(-1).broadcast_to([128, nb, 16]),
                        op=Op.mult).then_inc(sem_h[qv], 1)
    V(v_part2)
    P.inc(qv, 1)
    if wait_list is not None:
        wait_list.append((qv, P.val(qv)))
    return P.val(qv)


def emit_quant_v(P, sem_h, src_ap, FD, tiles, gs6_pp, gs2_pp, vq_ap,
                 amax_src=None):
    nb = FD // 16
    amax = tiles["amax"]; scb = tiles["scb"]; ebi = tiles["ebi"]; ebf = tiles["ebf"]
    r8i = tiles["r8i"]; m8 = tiles["m8"]; n8a = tiles["n8a"]; n8b = tiles["n8b"]
    p8i = tiles["p8i"]; sc8 = tiles["sc8"]; rcp = tiles["rcp"]
    rcp2 = tiles["rcp2"]; sc8h = tiles["sc8h"]
    tm = tiles["tm"]; at = tiles["at"]; qa = tiles["qa"]; qb = tiles["qb"]
    ga = tiles["ga"]; gb = tiles["gb"]

    nrb_am = P.inc("rb", 2)

    def v_part1(e, src_ap=src_ap, amax_src=amax_src, nrb=nrb_am):
        if amax_src is None:
            e.tensor_reduce(out=amax[:, 0:nb],
                            in_=src_ap.rearrange("p (a b) -> p a b", b=16),
                            axis=AX, op=Op.max,
                            apply_absolute_value=True).then_inc(sem_h["rb"], 1)
            e.wait_ge(sem_h["rb"], nrb - 1)
            am = amax[:, 0:nb]
        else:
            e.nop().then_inc(sem_h["rb"], 1)
            am = amax_src
        e.tensor_scalar(out=scb[:, 0:nb], in0=am, scalar1=gs6_pp,
                        scalar2=448.0, op0=Op.mult, op1=Op.min)
        e.tensor_scalar(out=ebi[:, 0:nb], in0=scb[:, 0:nb].bitcast(I32),
                        scalar1=EXP_BITS, scalar2=None, op0=Op.bitwise_and)
        e.tensor_scalar(out=ebf[:, 0:nb], in0=ebi[:, 0:nb].bitcast(F32),
                        scalar1=float(2.0 ** -6), scalar2=None, op0=Op.max)
        e.tensor_scalar(out=r8i[:, 0:nb], in0=ebf[:, 0:nb].bitcast(I32),
                        scalar1=-1.0, scalar2=C_REC8, op0=Op.mult, op1=Op.add)
        e.tensor_scalar(out=p8i[:, 0:nb], in0=ebf[:, 0:nb].bitcast(I32),
                        scalar1=C_PM3, scalar2=None, op0=Op.add)
        e.tensor_tensor(out=m8[:, 0:nb], in0=scb[:, 0:nb],
                        in1=r8i[:, 0:nb].bitcast(F32), op=Op.mult)
        e.tensor_scalar(out=n8a[:, 0:nb], in0=m8[:, 0:nb], scalar1=8.5,
                        scalar2=None, op0=Op.is_ge)
        for k in range(1, 8):
            s_, d_ = (n8a, n8b) if k % 2 == 1 else (n8b, n8a)
            e.scalar_tensor_tensor(out=d_[:, 0:nb], in0=m8[:, 0:nb], scalar=8.5 + k,
                                   in1=s_[:, 0:nb], op0=Op.is_ge, op1=Op.add)
        e.scalar_tensor_tensor(out=sc8[:, 0:nb], in0=n8b[:, 0:nb], scalar=8.0,
                               in1=p8i[:, 0:nb].bitcast(F32), op0=Op.add, op1=Op.mult)
        e.reciprocal(out=rcp[:, 0:nb], in_=sc8[:, 0:nb]).then_inc(sem_h["rb"], 1)
        e.tensor_scalar(out=sc8h[:, 0:nb], in0=sc8[:, 0:nb], scalar1=0.5,
                        scalar2=None, op0=Op.mult)
        e.wait_ge(sem_h["rb"], nrb)
        e.tensor_scalar(out=rcp2[:, 0:nb], in0=rcp[:, 0:nb], scalar1=gs2_pp,
                        scalar2=None, op0=Op.mult)
        e.tensor_tensor(out=tm[:, 0:FD].rearrange("p (a b) -> p a b", b=16),
                        in0=src_ap.rearrange("p (a b) -> p a b", b=16),
                        in1=rcp2[:, 0:nb].unsqueeze(-1).broadcast_to([128, nb, 16]),
                        op=Op.mult).then_inc(sem_h["qv"], 1)
    P.on("vector", v_part1)
    n_qv1 = P.inc("qv", 1)

    def s_part(e, src_ap=src_ap, n=n_qv1):
        e.wait_ge(sem_h["qv"], n)
        e.activation(out=at[:, 0:FD], in_=tm[:, 0:FD], func=AF.Abs)
        e.activation(out=tm[:, 0:FD], in_=src_ap, func=AF.Sign).then_inc(sem_h["qs"], 1)
    P.on("scalar", s_part)
    n_qs = P.inc("qs", 1)

    def v_part2(e, ns=n_qs, vq_ap=vq_ap):
        e.wait_ge(sem_h["qs"], ns)
        e.tensor_scalar(out=ga[:, 0:FD], in0=at[:, 0:FD], scalar1=10.0, scalar2=2.0,
                        op0=Op.is_ge, op1=Op.mult)
        e.scalar_tensor_tensor(out=gb[:, 0:FD], in0=at[:, 0:FD], scalar=7.0,
                               in1=ga[:, 0:FD], op0=Op.is_ge, op1=Op.add)
        e.scalar_tensor_tensor(out=ga[:, 0:FD], in0=at[:, 0:FD], scalar=5.0,
                               in1=gb[:, 0:FD], op0=Op.is_ge, op1=Op.add)
        e.tensor_scalar(out=qa[:, 0:FD], in0=at[:, 0:FD], scalar1=0.5, scalar2=None,
                        op0=Op.is_ge)
        for i, thr in enumerate((1.5, 2.5, 3.5)):
            s_, d_ = (qa, qb) if i % 2 == 0 else (qb, qa)
            e.scalar_tensor_tensor(out=d_[:, 0:FD], in0=at[:, 0:FD], scalar=thr,
                                   in1=s_[:, 0:FD], op0=Op.is_ge, op1=Op.add)
        e.scalar_tensor_tensor(out=qa[:, 0:FD], in0=ga[:, 0:FD], scalar=2.0,
                               in1=qb[:, 0:FD], op0=Op.mult, op1=Op.add)
        e.tensor_tensor(out=qb[:, 0:FD], in0=qa[:, 0:FD], in1=tm[:, 0:FD], op=Op.mult)
        e.tensor_tensor(out=vq_ap.rearrange("p (a b) -> p a b", b=16),
                        in0=qb[:, 0:FD].rearrange("p (a b) -> p a b", b=16),
                        in1=sc8h[:, 0:nb].unsqueeze(-1).broadcast_to([128, nb, 16]),
                        op=Op.mult).then_inc(sem_h["qv"], 1)
    P.on("vector", v_part2)
    return P.inc("qv", 1)


class TV:

    def __init__(self, base, p0, np_, c0, cols, dt):
        self.base, self.p0, self.np_, self.c0, self.cols, self.dt = base, p0, np_, c0, cols, dt

    def ap(self):
        a = self.base[self.p0:self.p0 + self.np_, self.c0:self.c0 + self.cols]
        if self.dt != F32:
            a = a.bitcast(self.dt)
        return a

    def __getitem__(self, sl):
        return self.ap()[sl]


class Arena:
    def __init__(self, base):
        self.base = base
        self.off = {}

    def at(self, c0_f32, np_, free_elems, dt, p0=0):
        itemsz = {F32: 4, I32: 4, BF16: 2}[dt]
        cols_f32 = (free_elems * itemsz + 3) // 4
        return TV(self.base, p0, np_, c0_f32, cols_f32, dt)


def build_core(dbg_phase="all"):
    nc = bass.Bass("TRN2", target_bir_lowering=False, detect_race_conditions=False)
    P = Prog(nc)

    ap = {}
    def din(name, shape, dt=F32):
        ap[name] = nc.dram_tensor(name, shape, dt, kind="ExternalInput").ap()
    din("x", [T, D]); din("wxbcT", [128, NC * 192]); din("wgT", [128, NC * 64])
    din("wlogAT", [128, NC * 16]); din("bBs", [128, 16]); din("bCs", [128, 16])
    din("bXs", [128, 16]); din("bgrep", [128, H * 64]); din("woT", [128, NC * D])
    din("borep", [128, D]); din("ident", [128, 128]); din("identb", [128, 128], BF16)
    din("tri", [128, 128]); din("ones128", [128, 128]); din("blogA", [16, 1])
    din("sel16", [16, 16 * 128])
    din("z_16x128", [16, 128]); din("z_64f", [64, 1024]); din("z_64b", [64, 1024], BF16)
    din("z_stmp", [1, 32]); din("epsrep", [128, 1])
    din("gate_w", [I_FF, D]); din("up_w", [I_FF, D]); din("down_w", [D, I_FF])
    din("s_in", [1, 1]); din("s_in_down", [1, 1])

    dbg = dbg_phase != "all"
    def scr(name, shape, dt):
        kind = "ExternalOutput" if dbg else "Internal"
        ap[name] = nc.dram_tensor(name, shape, dt, kind=kind).ap()
    scr("xn_scr", [T, D], BF16); scr("vqxn_scr", [T, D], BF16)
    scr("yg_scr", [T, D], BF16); scr("attn_scr", [T, D], F32)
    scr("vqh_scr", [TSUP, I_FF], BF16)
    scr("wq_g", [I_FF, D], BF16); scr("wq_u", [I_FF, D], BF16)
    scr("wq_d", [D, I_FF], BF16)
    ap["y"] = nc.dram_tensor("y", [T, D], F32, kind="ExternalOutput").ap()
    if dbg:
        ap["dbg_dec"] = nc.dram_tensor("dbg_dec", [16, 2 * T], F32, kind="ExternalOutput").ap()
        ap["dbg_t"] = nc.dram_tensor("dbg_t", [128, 2304], F32, kind="ExternalOutput").ap()
        ap["dbg_pp"] = nc.dram_tensor("dbg_pp", [128, 48], F32, kind="ExternalOutput").ap()

    es = ExitStack()
    with es:
        def sb(name, shape, dt=F32):
            return es.enter_context(nc.sbuf_tensor(name, shape, dt))

        semh = {}
        for s_ in ["sx", "sw", "scr1", "str1", "sout", "qv", "qs", "qg",
                   "lv", "ls", "tb", "vb", "sb_", "m4", "f5", "gq", "rb",
                   "gy", "go", "sxa"]:
            semh[s_] = es.enter_context(nc.semaphore(s_))

        L = {"semh": semh}

        L["c_wxbcTb"] = sb("c_wxbcTb", [128, NC * 192], BF16)
        L["c_wgTb"] = sb("c_wgTb", [128, NC * 64], BF16)
        L["c_wlogATb"] = sb("c_wlogATb", [128, NC * 16], BF16)
        L["c_bBs"] = sb("c_bBs", [128, 16]); L["c_bCs"] = sb("c_bCs", [128, 16])
        L["c_bXs"] = sb("c_bXs", [128, 16]); L["c_bgrep"] = sb("c_bgrep", [128, H * 64])
        L["c_woTb"] = sb("c_woTb", [128, NC * D], BF16)
        L["c_borep"] = sb("c_borep", [128, D])
        L["c_ident"] = sb("c_ident", [128, 128]); L["c_identb"] = sb("c_identb", [128, 128], BF16)
        L["c_tri"] = sb("c_tri", [128, 128]); L["c_ones"] = sb("c_ones", [128, 128])
        L["c_blogA"] = sb("c_blogA", [16, 1])
        L["c_sx"] = sb("c_sx", [1, 1]); L["c_sxd"] = sb("c_sxd", [1, 1])
        L["PP"] = sb("PP", [128, 16])
        L["stmp"] = sb("stmp", [1, 32])
        L["c_eps"] = sb("c_eps", [128, 1])
        L["gmx"] = sb("gmx", [128, 4])

        QF = 1024
        qt = {}
        for n_, fe, dt in [("amax", QF // 16, F32), ("scb", QF // 16, F32),
                           ("ebi", QF // 16, I32), ("ebf", QF // 16, F32),
                           ("r8i", QF // 16, I32), ("m8", QF // 16, F32),
                           ("n8a", QF // 16, F32), ("n8b", QF // 16, F32),
                           ("p8i", QF // 16, I32), ("sc8", QF // 16, F32),
                           ("rcp", QF // 16, F32), ("rcp2", QF // 16, F32),
                           ("sc8h", QF // 16, F32), ("tm", QF, F32),
                           ("at", QF, F32), ("qa", QF, F32),
                           ("qb", QF, F32), ("ga", QF, F32), ("gb", QF, F32)]:
            qt[n_] = sb("qt_" + n_, [128, fe], dt)
        L["qt"] = qt

        L["logA_sb"] = sb("logA_sb", [16, T]); L["dec_sb"] = sb("dec_sb", [16, T])
        L["ecT_sb"] = sb("ecT_sb", [16, T]); L["zz16"] = sb("zz16", [16, 128])
        L["Bt2"] = sb("Bt2", [64, 256], BF16); L["Ct2"] = sb("Ct2", [64, 256], BF16)
        L["Xt2"] = sb("Xt2", [64, 256], BF16); L["Dt2"] = sb("Dt2", [128, 256], F32)
        L["Ptmp"] = sb("Ptmp", [128, 128], F32); L["Pm2"] = sb("Pm2", [128, 256], BF16)
        L["Xk2"] = sb("Xk2", [128, 128], BF16); L["Bsc2"] = sb("Bsc2", [128, 128], BF16)
        L["Csc2"] = sb("Csc2", [64, 256], BF16); L["gate_s2"] = sb("gate_s2", [128, 128], F32)
        L["S_sb"] = sb("S_sb", [64, H * 64]); L["Sb_sb"] = sb("Sb_sb", [64, H * 64], BF16)
        L["decColT2"] = sb("decColT2", [128, 32]); L["negdecT2"] = sb("negdecT2", [128, 32])
        L["ebpT2"] = sb("ebpT2", [128, 32]); L["lamrow2"] = sb("lamrow2", [128, 32])
        L["lamT2"] = sb("lamT2", [128, 32])
        L["Stmp"] = sb("Stmp", [64, 64])
        L["diag16"] = sb("diag16", [16, 16])
        L["wred"] = sb("wred", [128, 1])
        L["ln_s"] = sb("ln_s", [128, 8])

        AR_COLS = 28672
        arena = sb("arena", [128, AR_COLS], F32)
        A = Arena(arena)
        VQXNT0 = 0
        XNT0 = 8192
        C0 = 16384
        L["vqxnT"] = A.at(VQXNT0, 128, NC * T, BF16)
        L["xnT"] = A.at(XNT0, 128, NC * T, BF16)
        L["ygT"] = A.at(XNT0, 128, NC * T, BF16)
        L["vqhT"] = A.at(XNT0, 128, 32 * 512, BF16)
        L["st_woT"] = A.at(C0, 128, NC * D, F32)
        L["st_wxbcT"] = A.at(C0 + 8192, 128, NC * 192, F32)
        L["st_wgT"] = A.at(C0 + 9728, 128, NC * 64, F32)
        L["st_wlogAT"] = A.at(XNT0, 128, NC * 16, F32)
        L["xtile"] = A.at(C0, 128, 2 * D, F32)
        L["xc_t"] = A.at(C0 + 2048, 128, D, F32)
        L["sq_t"] = A.at(C0 + 3072, 128, D, F32)
        L["xn_t"] = A.at(C0 + 4096, 128, D, F32)
        L["xnb_t"] = A.at(C0 + 5120, 128, 2 * D, BF16)
        L["vqxn_t"] = A.at(C0 + 6144, 128, 2 * D, BF16)
        L["wtile"] = A.at(C0, 128, 2 * 4096, F32)
        L["wvq"] = A.at(C0 + 8192, 128, 4096, BF16)
        L["wamax"] = A.at(XNT0, 128, 6144, F32)
        L["sel16"] = A.at(C0 + 10240, 16, 2048, F32)
        L["Yg"] = A.at(C0, 128, NT * D, BF16)
        L["ao_t"] = A.at(C0 + 8192, 128, 2 * D, F32)
        L["hstage"] = [A.at(C0 + t * 1024, 128, D, F32) for t in range(4)]
        L["sgt"] = A.at(C0 + 4096, 128, 2 * 512, F32)
        L["gwb"] = A.at(C0 + 5120, 128, 2 * 512, BF16)
        L["uwb"] = A.at(C0 + 5632, 128, 2 * 512, BF16)
        L["dwb"] = A.at(C0 + 6144, 128, 2 * 512, BF16)
        L["xat"] = A.at(C0 + 6656, 128, 2 * D, F32)
        L["aat"] = A.at(C0 + 8704, 128, 2 * D, F32)
        L["out_t"] = A.at(C0 + 10752, 128, 2 * 512, F32)
        L["vqh_t"] = A.at(C0 + 11776, 128, 0, BF16)

        bank = [es.enter_context(nc.psum_tensor(f"bank{i}", [128, 512], F32))
                for i in range(8)]
        L["bank"] = bank

        block = es.enter_context(nc.Block())
        emit_all(P, ap, L, dbg_phase)

        @block.sync
        def _(e):
            for f in P.q["sync"]:
                f(e)

        @block.scalar
        def _(e):
            for f in P.q["scalar"]:
                f(e)

        @block.vector
        def _(e):
            for f in P.q["vector"]:
                f(e)

        @block.tensor
        def _(e):
            for f in P.q["tensor"]:
                f(e)

        @block.gpsimd
        def _(e):
            for f in P.q["gpsimd"]:
                f(e)
    return nc


SKIP_QUANT = False


def emit_all(P, ap, L, dbg_phase):
    semh = L["semh"]
    bank = L["bank"]

    def c_load(e):
        pairs = [("wxbcT", L["st_wxbcT"]), ("wgT", L["st_wgT"]), ("wlogAT", L["st_wlogAT"]),
                 ("woT", L["st_woT"]),
                 ("bBs", L["c_bBs"]), ("bCs", L["c_bCs"]), ("bXs", L["c_bXs"]),
                 ("bgrep", L["c_bgrep"]), ("borep", L["c_borep"]),
                 ("ident", L["c_ident"]), ("identb", L["c_identb"]), ("tri", L["c_tri"]),
                 ("ones128", L["c_ones"]), ("blogA", L["c_blogA"]),
                 ("s_in", L["c_sx"]), ("s_in_down", L["c_sxd"]), ("sel16", L["sel16"]),
                 ("z_16x128", L["zz16"]), ("z_64f", L["S_sb"]), ("z_64b", L["Sb_sb"]),
                 ("z_stmp", L["stmp"]), ("epsrep", L["c_eps"])]
        for nm, dst in pairs:
            e.dma_start(out=dst[:], in_=ap[nm][:, :]).then_inc(semh["sw"], 16)
    P.on("sync", c_load)
    n_const = P.inc("sw", 16 * 22)

    def c_cast(e, n=n_const):
        e.wait_ge(semh["sw"], n)
        e.tensor_copy(out=L["c_wxbcTb"][:], in_=L["st_wxbcT"][:])
        e.tensor_copy(out=L["c_wgTb"][:], in_=L["st_wgT"][:])
        e.tensor_copy(out=L["c_wlogATb"][:], in_=L["st_wlogAT"][:])
        e.tensor_copy(out=L["c_woTb"][:], in_=L["st_woT"][:])
        e.tensor_scalar(out=L["stmp"][0:1, 0:1], in0=L["c_sx"][0:1, 0:1],
                        scalar1=1.0 / 6.0, scalar2=None, op0=Op.mult)
        e.tensor_scalar(out=L["stmp"][0:1, 1:2], in0=L["c_sx"][0:1, 0:1],
                        scalar1=2.0, scalar2=None, op0=Op.mult).then_inc(semh["lv"], 1)
    P.on("vector", c_cast)
    n_cast = P.inc("lv", 1)

    def c_bcast_mm(e, n=n_cast):
        e.wait_ge(semh["lv"], n)
        e.matmul(out=bank[7][:, 0:16], lhsT=L["c_ones"][0:1, 0:128],
                 rhs=L["stmp"][0:1, 0:16], start=True, stop=True).then_inc(semh["tb"], 1)
    P.on("tensor", c_bcast_mm)
    n_bcmm = P.inc("tb", 1)

    def c_bcast_cp(e, n=n_bcmm):
        e.wait_ge(semh["tb"], n)
        e.tensor_copy(out=L["PP"][:], in_=bank[7][:, 0:16]).then_inc(semh["lv"], 1)
    P.on("vector", c_bcast_cp)
    n_bc_x = P.inc("lv", 1)

    xtile = L["xtile"]; xc_t = L["xc_t"]; sq_t = L["sq_t"]; xn_t = L["xn_t"]
    xnb_t = L["xnb_t"]; vqxn_t = L["vqxn_t"]; ln_s = L["ln_s"]

    for i in range(NT):
        xs = xtile[:, (i % 2) * D:(i % 2 + 1) * D]
        xnb = xnb_t[:, (i % 2) * D:(i % 2 + 1) * D]
        vqx = vqxn_t[:, (i % 2) * D:(i % 2 + 1) * D]

        def s_load(e, i=i, xs=xs, ncast=n_cast):
            if i < 2:
                e.wait_ge(semh["lv"], ncast)
            else:
                e.wait_ge(semh["ls"], 3 * (i - 2) + 1)
            e.dma_start(out=xs, in_=ap["x"][i * 128:(i + 1) * 128, :]).then_inc(semh["sx"], 16)
        P.on("sync", s_load)
        n_x = P.inc("sx", 16)

        nrb_ln = P.inc("rb", 1)
        def v_ln1(e, i=i, xs=xs, n=n_x, nrb=nrb_ln):
            e.wait_ge(semh["sx"], n)
            e.tensor_reduce(out=ln_s[:, 0:1], in_=xs, axis=AX, op=Op.add).then_inc(semh["rb"], 1)
            e.wait_ge(semh["rb"], nrb)
            e.tensor_scalar(out=ln_s[:, 1:2], in0=ln_s[:, 0:1], scalar1=-1.0 / D,
                            scalar2=None, op0=Op.mult).then_inc(semh["lv"], 1)
        P.on("vector", v_ln1)
        a_v = P.inc("lv", 1)

        def s_ln1(e, xs=xs, n=a_v):
            e.wait_ge(semh["lv"], n)
            e.activation(out=xc_t[:], in_=xs, func=AF.Identity, bias=ln_s[:, 1:2], scale=1.0)
            e.activation(out=sq_t[:], in_=xc_t[:], func=AF.Square).then_inc(semh["ls"], 1)
        P.on("scalar", s_ln1)
        a_s = P.inc("ls", 1)

        def v_ln2(e, n=a_s):
            e.wait_ge(semh["ls"], n)
            e.tensor_reduce(out=ln_s[:, 2:3], in_=sq_t[:], axis=AX, op=Op.add).then_inc(semh["lv"], 1)
        P.on("vector", v_ln2)
        b_v = P.inc("lv", 1)

        def s_ln2(e, n=b_v):
            e.wait_ge(semh["lv"], n)
            e.activation(out=ln_s[:, 3:4], in_=ln_s[:, 2:3], func=AF.Sqrt,
                         bias=L["c_eps"][:], scale=1.0 / D).then_inc(semh["ls"], 1)
        P.on("scalar", s_ln2)
        b_s = P.inc("ls", 1)

        def v_ln3(e, n=b_s):
            e.wait_ge(semh["ls"], n)
            e.reciprocal(out=ln_s[:, 0:1], in_=ln_s[:, 3:4]).then_inc(semh["lv"], 1)
        P.on("vector", v_ln3)
        c_v = P.inc("lv", 1)

        def s_ln3(e, n=c_v):
            e.wait_ge(semh["lv"], n)
            e.activation(out=xn_t[:], in_=xc_t[:], func=AF.Identity,
                         bias=0.0, scale=ln_s[:, 0:1]).then_inc(semh["ls"], 1)
        P.on("scalar", s_ln3)
        c_s = P.inc("ls", 1)

        def v_xnb(e, i=i, xnb=xnb, n=c_s):
            e.wait_ge(semh["ls"], n)
            if i >= 2:
                e.wait_ge(semh["scr1"], 16 * (2 * (i - 2) + 1))
            e.tensor_copy(out=xnb, in_=xn_t[:]).then_inc(semh["lv"], 1)
        P.on("vector", v_xnb)
        d_v = P.inc("lv", 1)

        def v_qpre(e, i=i, n=d_v):
            if i >= 2:
                e.wait_ge(semh["scr1"], 16 * (2 * (i - 2) + 2))
        P.on("vector", v_qpre)
        if SKIP_QUANT:
            def v_qfake(e, vqx=vqx):
                e.tensor_copy(out=vqx, in_=xn_t[:]).then_inc(semh["qv"], 2)
            P.on("vector", v_qfake)
            n_vq = P.inc("qv", 2)
        else:
            n_vq = emit_quant_v(P, semh, xn_t[:], D, L["qt"],
                                L["PP"][:, 0:1], L["PP"][:, 1:2], vqx)

        def s_dma(e, i=i, xnb=xnb, vqx=vqx, nd=d_v, nq=n_vq):
            e.wait_ge(semh["lv"], nd)
            e.dma_start(out=ap["xn_scr"][i * 128:(i + 1) * 128, :], in_=xnb).then_inc(semh["scr1"], 16)
            e.wait_ge(semh["qv"], nq)
            e.dma_start(out=ap["vqxn_scr"][i * 128:(i + 1) * 128, :], in_=vqx).then_inc(semh["scr1"], 16)
        P.on("sync", s_dma)
        P.inc("scr1", 32)

    def s_tr(e, n=P.val("scr1")):
        e.wait_ge(semh["scr1"], n)
        for c in range(NC):
            e.dma_start(out=L["xnT"][:, c * T:(c + 1) * T],
                        in_=ap["xn_scr"][:, c * 128:(c + 1) * 128],
                        transpose=True).then_inc(semh["str1"], 16)
        for c in range(NC):
            e.dma_start(out=L["vqxnT"][:, c * T:(c + 1) * T],
                        in_=ap["vqxn_scr"][:, c * 128:(c + 1) * 128],
                        transpose=True).then_inc(semh["str1"], 16)
    P.on("sync", s_tr)
    n_tr1 = P.inc("str1", 16 * 16)

    if dbg_phase == "ln":
        def g_fin(e, n=P.val("scr1"), n2=n_tr1):
            e.wait_ge(semh["scr1"], n)
            e.wait_ge(semh["str1"], n2)
        P.on("gpsimd", g_fin)
        return

    emit_mamba(P, ap, L, dbg_phase)
    if dbg_phase == "mamba":
        return
    emit_ffn(P, ap, L, dbg_phase)


def emit_mamba(P, ap, L, dbg_phase):
    semh = L["semh"]; bank = L["bank"]
    xnT = L["xnT"]; dec = L["dec_sb"]; logA = L["logA_sb"]; ecT = L["ecT_sb"]
    Bt2, Ct2, Xt2 = L["Bt2"], L["Ct2"], L["Xt2"]
    Dt2, Ptmp, Pm2 = L["Dt2"], L["Ptmp"], L["Pm2"]
    Xk2, Bsc2, Csc2, gate_s2 = L["Xk2"], L["Bsc2"], L["Csc2"], L["gate_s2"]
    S_sb, Sb_sb, Stmp = L["S_sb"], L["Sb_sb"], L["Stmp"]
    decColT2, negdecT2, ebpT2 = L["decColT2"], L["negdecT2"], L["ebpT2"]
    lamrow2, lamT2 = L["lamrow2"], L["lamT2"]
    Yg = L["Yg"]

    n_xnT_tr = 16 * 8

    for tc in range(4):
        def t_loga(e, tc=tc, w=P.val("ls")):
            if tc == 0:
                e.wait_ge(semh["str1"], n_xnT_tr)
            else:
                e.wait_ge(semh["ls"], w)
            for c in range(NC):
                mm = e.matmul(out=bank[0][0:16, 0:512],
                              lhsT=L["c_wlogATb"][:, c * 16:(c + 1) * 16],
                              rhs=xnT[:, c * T + tc * 512: c * T + (tc + 1) * 512],
                              start=(c == 0), stop=(c == NC - 1))
                if c == NC - 1:
                    mm.then_inc(semh["tb"], 1)
        P.on("tensor", t_loga)
        n_t = P.inc("tb", 1)

        def s_loga(e, tc=tc, n=n_t):
            e.wait_ge(semh["tb"], n)
            e.activation(out=logA[0:16, tc * 512:(tc + 1) * 512], in_=bank[0][0:16, 0:512],
                         func=AF.Identity, bias=L["c_blogA"][0:16, 0:1],
                         scale=1.0).then_inc(semh["ls"], 1)
        P.on("scalar", s_loga)
        P.inc("ls", 1)

    n_loga = P.val("ls")

    def v_dec(e, n=n_loga):
        e.wait_ge(semh["ls"], n)
        for k in range(16):
            li = e.tensor_tensor_scan(out=dec[0:16, k * 128:(k + 1) * 128],
                                      data0=logA[0:16, k * 128:(k + 1) * 128],
                                      data1=L["zz16"][0:16, 0:128], initial=0.0,
                                      op0=Op.add, op1=Op.add)
        li.then_inc(semh["lv"], 1)
    P.on("vector", v_dec)
    n_dec = P.inc("lv", 1)

    def s_ec(e, n=n_dec):
        e.wait_ge(semh["lv"], n)
        e.activation(out=ecT[0:16, :], in_=dec[0:16, :], func=AF.Exp).then_inc(semh["ls"], 1)
    P.on("scalar", s_ec)
    n_ec = P.inc("ls", 1)

    tb0 = P.val("tb"); vb0 = P.val("vb"); sb0 = P.val("sb_")
    NB = 259

    wx = L["c_wxbcTb"]; wg = L["c_wgTb"]

    for B in range(NB):
        def t_beat(e, B=B, tb0=tb0, vb0=vb0, sb0=sb0, ndec=n_dec, nec=n_ec):
            if B == 0:
                e.wait_ge(semh["lv"], ndec)
                e.wait_ge(semh["ls"], nec)
            if B >= 1:
                e.wait_ge(semh["vb"], vb0 + B)
                e.wait_ge(semh["sb_"], sb0 + B)
            pb = B % 2
            li = None
            if B % 16 == 0 and B < 256:
                k = B // 16
                li = e.transpose(out=bank[2 + pb][:, 384:400],
                            in_=dec[0:16, k * 128:(k + 1) * 128],
                            identity=L["c_ident"][0:16, 0:16])
            if B >= 2 and (B - 2) % 16 == 0 and (B - 2) // 16 < 16:
                k = (B - 2) // 16
                li = e.matmul(out=bank[2 + pb][:, 400:416], lhsT=L["c_ones"][0:16, 0:128],
                         rhs=L["diag16"][0:16, 0:16], start=True, stop=True)
            j3 = B - 3
            if 0 <= j3 < 256:
                k, h = divmod(j3, 16); pj = j3 % 2
                li = e.matmul(out=bank[6 + pb][:, 0:64], lhsT=Pm2[:, pj * 128:(pj + 1) * 128],
                         rhs=Xk2[:, pj * 64:(pj + 1) * 64], start=True, stop=False)
                li = e.matmul(out=bank[6 + pb][:, 0:64], lhsT=Csc2[0:64, pj * 128:(pj + 1) * 128],
                         rhs=Sb_sb[0:64, h * 64:(h + 1) * 64], start=False, stop=True)
                li = e.matmul(out=bank[6 + pb][0:64, 64:128], lhsT=Bsc2[:, pj * 64:(pj + 1) * 64],
                         rhs=Xk2[:, pj * 64:(pj + 1) * 64], start=True, stop=True)
            j2 = B - 2
            if 0 <= j2 < 256:
                k, h = divmod(j2, 16); pj = j2 % 2
                tb_v = bank[4 + pb][:, 0:32].bitcast(BF16)
                tx_v = bank[4 + pb][:, 32:64].bitcast(BF16)
                li = e.transpose(out=tb_v, in_=Bt2[0:64, pj * 128:(pj + 1) * 128],
                            identity=L["c_identb"][0:64, 0:64])
                li = e.transpose(out=tx_v, in_=Xt2[0:64, pj * 128:(pj + 1) * 128],
                            identity=L["c_identb"][0:64, 0:64])
                li = e.matmul(out=bank[2 + pb][:, 0:128],
                         lhsT=L["sel16"][0:16, h * 128:(h + 1) * 128],
                         rhs=dec[0:16, k * 128:(k + 1) * 128], start=True, stop=True)
                li = e.matmul(out=bank[2 + pb][:, 128:256], lhsT=Bt2[0:64, pj * 128:(pj + 1) * 128],
                         rhs=Ct2[0:64, pj * 128:(pj + 1) * 128], start=True, stop=True)
                li = e.matmul(out=bank[2 + pb][0:64, 256:384],
                         lhsT=L["sel16"][0:16, h * 128: h * 128 + 64],
                         rhs=ecT[0:16, k * 128:(k + 1) * 128], start=True, stop=True)
            j1 = B - 1
            if 0 <= j1 < 256:
                k, h = divmod(j1, 16); pj = j1 % 2
                cc = h // 2; po = 64 * (h % 2)
                xn_sl = xnT[po:po + 64, cc * T + k * 128: cc * T + (k + 1) * 128]
                li = e.matmul(out=bank[pb][0:64, 0:128],
                         lhsT=wx[po:po + 64, cc * 192 + 64: cc * 192 + 128],
                         rhs=xn_sl, start=True, stop=True)
                li = e.matmul(out=bank[pb][0:64, 128:256],
                         lhsT=wx[po:po + 64, cc * 192 + 128: cc * 192 + 192],
                         rhs=xn_sl, start=True, stop=True)
                li = e.matmul(out=bank[pb][0:64, 256:384],
                         lhsT=wx[po:po + 64, cc * 192: cc * 192 + 64],
                         rhs=xn_sl, start=True, stop=True)
                li = e.matmul(out=bank[pb][:, 384:448], lhsT=xn_sl,
                         rhs=wg[po:po + 64, cc * 64:(cc + 1) * 64], start=True, stop=True)
            (li if li is not None else e.nop()).then_inc(semh["tb"], 1)
        P.on("tensor", t_beat)
        P.inc("tb", 1)

        def s_beat(e, B=B, tb0=tb0, vb0=vb0):
            e.wait_ge(semh["tb"], tb0 + B + 1)
            if B >= 1:
                e.wait_ge(semh["vb"], vb0 + B)
            pb = B % 2
            li = None
            if B >= 1 and (B - 1) % 16 == 0 and (B - 1) // 16 < 16:
                k = (B - 1) // 16; pk = k % 2
                li = e.activation(out=ebpT2[:, pk * 16:(pk + 1) * 16],
                             in_=negdecT2[:, pk * 16:(pk + 1) * 16], func=AF.Exp)
            if B >= 2 and (B - 2) % 16 == 0 and (B - 2) // 16 < 16:
                k = (B - 2) // 16; pk = k % 2
                li = e.activation(out=lamT2[:, pk * 16:(pk + 1) * 16],
                             in_=bank[2 + pb][:, 400:416], func=AF.Exp)
            j1 = B - 1
            if 0 <= j1 < 256:
                k, h = divmod(j1, 16); pj = j1 % 2
                li = e.activation(out=Bt2[0:64, pj * 128:(pj + 1) * 128], in_=bank[pb][0:64, 0:128],
                             func=AF.Silu, bias=L["c_bBs"][0:64, h:h + 1], scale=1.0)
                li = e.activation(out=Ct2[0:64, pj * 128:(pj + 1) * 128], in_=bank[pb][0:64, 128:256],
                             func=AF.Silu, bias=L["c_bCs"][0:64, h:h + 1], scale=1.0)
                li = e.activation(out=Xt2[0:64, pj * 128:(pj + 1) * 128], in_=bank[pb][0:64, 256:384],
                             func=AF.Silu, bias=L["c_bXs"][0:64, h:h + 1], scale=1.0)
            j2 = B - 2
            if 0 <= j2 < 256:
                k, h = divmod(j2, 16); pj = j2 % 2; pk = k % 2
                li = e.activation(out=Dt2[:, pj * 128:(pj + 1) * 128], in_=bank[2 + pb][:, 0:128],
                             func=AF.Exp, bias=negdecT2[:, pk * 16 + h: pk * 16 + h + 1],
                             scale=1.0)
            (li if li is not None else e.nop()).then_inc(semh["sb_"], 1)
        P.on("scalar", s_beat)
        P.inc("sb_", 1)

        nrb_ch = P.inc("rb", 1) if (B % 16 == 0 and B < 256) else None
        def v_beat(e, B=B, tb0=tb0, sb0=sb0, nrb=nrb_ch):
            e.wait_ge(semh["tb"], tb0 + B + 1)
            e.wait_ge(semh["sb_"], sb0 + B + 1)
            pb = B % 2
            li = None
            j3 = B - 3
            if 0 <= j3 < 256:
                k, h = divmod(j3, 16); pj = j3 % 2; pk = k % 2
                li = e.tensor_tensor(out=Yg[:, k * D + h * 64: k * D + (h + 1) * 64],
                                in0=gate_s2[:, pj * 64:(pj + 1) * 64],
                                in1=bank[6 + pb][:, 0:64], op=Op.mult)
                li = e.tensor_tensor(out=Stmp[0:64, 0:64], in0=S_sb[0:64, h * 64:(h + 1) * 64],
                                in1=bank[6 + pb][0:64, 64:128], op=Op.add)
            if B % 16 == 0 and B < 256:
                k = B // 16; pk = k % 2
                li = e.tensor_copy(out=decColT2[:, pk * 16:(pk + 1) * 16],
                                   in_=bank[2 + pb][:, 384:400])
                li.then_inc(semh["rb"], 1)
                li = e.tensor_scalar(out=L["diag16"][0:16, 0:16], in0=L["c_ident"][0:16, 0:16],
                                scalar1=dec[0:16, k * 128 + 127: k * 128 + 128],
                                scalar2=None, op0=Op.mult)
                e.wait_ge(semh["rb"], nrb)
                li = e.tensor_scalar(out=negdecT2[:, pk * 16:(pk + 1) * 16],
                                in0=decColT2[:, pk * 16:(pk + 1) * 16],
                                scalar1=-1.0, scalar2=None, op0=Op.mult)
            j2 = B - 2
            if 0 <= j2 < 256:
                k, h = divmod(j2, 16); pj = j2 % 2; pk = k % 2
                li = e.tensor_tensor(out=Ptmp[:], in0=Dt2[:, pj * 128:(pj + 1) * 128],
                                in1=bank[2 + pb][:, 128:256], op=Op.mult)
                li = e.tensor_scalar(out=Bsc2[:, pj * 64:(pj + 1) * 64],
                                in0=bank[4 + pb][:, 0:32].bitcast(BF16),
                                scalar1=ebpT2[:, pk * 16 + h: pk * 16 + h + 1],
                                scalar2=None, op0=Op.mult)
                li = e.tensor_copy(out=Xk2[:, pj * 64:(pj + 1) * 64],
                              in_=bank[4 + pb][:, 32:64].bitcast(BF16))
                li = e.tensor_tensor(out=Csc2[0:64, pj * 128:(pj + 1) * 128],
                                in0=Ct2[0:64, pj * 128:(pj + 1) * 128],
                                in1=bank[2 + pb][0:64, 256:384], op=Op.mult)
                li = e.tensor_tensor(out=Pm2[:, pj * 128:(pj + 1) * 128], in0=Ptmp[:],
                                in1=L["c_tri"][:], op=Op.mult)
            j1 = B - 1
            if 0 <= j1 < 256:
                k, h = divmod(j1, 16); pj = j1 % 2
                li = e.tensor_tensor(out=gate_s2[:, pj * 64:(pj + 1) * 64],
                                in0=bank[pb][:, 384:448],
                                in1=L["c_bgrep"][:, h * 64:(h + 1) * 64], op=Op.add)
            if 0 <= j3 < 256:
                k, h = divmod(j3, 16); pk = k % 2
                li = e.tensor_scalar(out=S_sb[0:64, h * 64:(h + 1) * 64], in0=Stmp[0:64, 0:64],
                                scalar1=lamT2[0:64, pk * 16 + h: pk * 16 + h + 1],
                                scalar2=None, op0=Op.mult)
                li = e.tensor_copy(out=Sb_sb[0:64, h * 64:(h + 1) * 64],
                              in_=S_sb[0:64, h * 64:(h + 1) * 64])
            (li if li is not None else e.nop()).then_inc(semh["vb"], 1)
        P.on("vector", v_beat)
        P.inc("vb", 1)

    n_scan_vb = P.val("vb")

    if dbg_phase == "mamba" and "dbg_dec" in ap:
        def g_dbg(e, n=n_scan_vb):
            e.wait_ge(semh["vb"], n)
            e.dma_start(out=ap["dbg_dec"][0:16, 0:T], in_=dec[0:16, :]).then_inc(semh["gq"], 16)
            e.dma_start(out=ap["dbg_dec"][0:16, T:2 * T], in_=logA[0:16, :]).then_inc(semh["gq"], 16)
            e.dma_start(out=ap["dbg_t"][:, 0:256], in_=Dt2[:]).then_inc(semh["gq"], 16)
            e.dma_start(out=ap["dbg_t"][:, 256:384],
                        in_=Pm2[:].bitcast(F32)).then_inc(semh["gq"], 16)
            e.dma_start(out=ap["dbg_t"][0:64, 384:512],
                        in_=Bt2[:].bitcast(F32)).then_inc(semh["gq"], 16)
            e.dma_start(out=ap["dbg_t"][0:64, 512:640],
                        in_=Ct2[:].bitcast(F32)).then_inc(semh["gq"], 16)
            e.dma_start(out=ap["dbg_t"][0:64, 640:768],
                        in_=Xt2[:].bitcast(F32)).then_inc(semh["gq"], 16)
            e.dma_start(out=ap["dbg_t"][:, 768:832],
                        in_=Xk2[:].bitcast(F32)).then_inc(semh["gq"], 16)
            e.dma_start(out=ap["dbg_t"][:, 832:896],
                        in_=Bsc2[:].bitcast(F32)).then_inc(semh["gq"], 16)
            e.dma_start(out=ap["dbg_t"][0:64, 896:1024],
                        in_=Csc2[:].bitcast(F32)).then_inc(semh["gq"], 16)
            e.dma_start(out=ap["dbg_t"][:, 1024:1152], in_=gate_s2[:]).then_inc(semh["gq"], 16)
            e.dma_start(out=ap["dbg_t"][0:64, 1152:2176],
                        in_=S_sb[:]).then_inc(semh["gq"], 16)
            e.dma_start(out=ap["dbg_t"][:, 2176:2208],
                        in_=L["ebpT2"][:]).then_inc(semh["gq"], 16)
            e.dma_start(out=ap["dbg_t"][:, 2208:2240],
                        in_=L["lamT2"][:]).then_inc(semh["gq"], 16)
            e.dma_start(out=ap["dbg_t"][:, 2240:2272],
                        in_=L["decColT2"][:]).then_inc(semh["gq"], 16)
        P.on("gpsimd", g_dbg)
        P.inc("gq", 16 * 15)

    def s_yg(e, n=n_scan_vb):
        e.wait_ge(semh["vb"], n)
        for i in range(NT):
            e.dma_start(out=ap["yg_scr"][i * 128:(i + 1) * 128, :],
                        in_=Yg[:, i * D:(i + 1) * D]).then_inc(semh["scr1"], 16)
    P.on("sync", s_yg)
    n_ygw = P.inc("scr1", 16 * NT)

    def s_ygT(e, n=n_ygw):
        e.wait_ge(semh["scr1"], n)
        for c in range(NC):
            e.dma_start(out=L["ygT"][:, c * T:(c + 1) * T],
                        in_=ap["yg_scr"][:, c * 128:(c + 1) * 128],
                        transpose=True).then_inc(semh["str1"], 16)
    P.on("sync", s_ygT)
    n_ygT = P.inc("str1", 16 * NC)

    sout0 = P.val("sout")
    for i in range(NT):
        def t_op(e, i=i, n=n_ygT, w=P.val("m4")):
            if i == 0:
                e.wait_ge(semh["str1"], n)
            if i >= 1:
                e.wait_ge(semh["m4"], w)
            for half in range(2):
                for c in range(NC):
                    li = e.matmul(out=bank[half][:, 0:512],
                                  lhsT=L["ygT"][:, c * T + i * 128: c * T + (i + 1) * 128],
                                  rhs=L["c_woTb"][:, c * D + half * 512: c * D + half * 512 + 512],
                                  start=(c == 0), stop=(c == NC - 1))
            li.then_inc(semh["tb"], 1)
        P.on("tensor", t_op)
        n_t = P.inc("tb", 1)

        def v_op(e, i=i, n=n_t, s0=sout0):
            e.wait_ge(semh["tb"], n)
            if i >= 2:
                e.wait_ge(semh["sout"], s0 + 16 * (i - 1))
            for half in range(2):
                li = e.scalar_tensor_tensor(out=L["ao_t"][:, (i % 2) * D + half * 512:(i % 2) * D + half * 512 + 512],
                                             in0=bank[half][:, 0:512], scalar=0.0,
                                             in1=L["c_borep"][:, half * 512:half * 512 + 512],
                                             op0=Op.add, op1=Op.add)
            li.then_inc(semh["m4"], 2)
        P.on("vector", v_op)
        n_v = P.inc("m4", 2)

        def s_st(e, i=i, n=n_v):
            e.wait_ge(semh["m4"], n)
            e.dma_start(out=ap["attn_scr"][i * 128:(i + 1) * 128, :],
                        in_=L["ao_t"][:, (i % 2) * D:(i % 2 + 1) * D]).then_inc(semh["sout"], 16)
        P.on("scalar", s_st)
        P.inc("sout", 16)

    if dbg_phase == "mamba":
        def g_fin(e, n=P.val("sout"), nq=P.val("gq")):
            e.wait_ge(semh["sout"], n)
            if nq:
                e.wait_ge(semh["gq"], nq)
        P.on("gpsimd", g_fin)


def emit_ffn(P, ap, L, dbg_phase):
    semh = L["semh"]; bank = L["bank"]
    PP = L["PP"]; qt = L["qt"]
    vqxnT = L["vqxnT"]; vqhT = L["vqhT"]
    wtile, wvq, wamax = L["wtile"], L["wvq"], L["wamax"]
    stmp = L["stmp"]

    n_ph4_done = P.val("sout")
    n_yg_done = P.val("scr1")

    W_CFG = [("gate_w", "wq_g", 32, 1024, 7, 8),
             ("up_w", "wq_u", 32, 1024, 9, 10),
             ("down_w", "wq_d", 8, 4096, 11, 12)]

    gti = 0
    rb_after = []
    for wi, (wn, qn, ntile, FDW, c6, c2) in enumerate(W_CFG):
        nbw = FDW // 16
        for ti in range(ntile):
            sl = wtile[:, (gti % 2) * 4096:(gti % 2) * 4096 + FDW]

            def s_wload(e, wn=wn, ti=ti, sl=sl, wfirst=(gti == 0),
                        wtgt=(rb_after[gti - 2] if gti >= 2 else None),
                        nyg=n_yg_done):
                if wfirst:
                    e.wait_ge(semh["scr1"], nyg)
                if wtgt is not None:
                    e.wait_ge(semh["rb"], wtgt)
                e.dma_start(out=sl, in_=ap[wn][ti * 128:(ti + 1) * 128, :]).then_inc(semh["sw"], 16)
            P.on("sync", s_wload)
            n_l = P.inc("sw", 16)

            def v_wamax(e, wi=wi, ti=ti, nbw=nbw, sl=sl, n=n_l, nrb=P.inc("rb", 1)):
                e.wait_ge(semh["sw"], n)
                e.tensor_reduce(out=wamax[:, wi * 2048 + ti * nbw:wi * 2048 + (ti + 1) * nbw],
                                in_=sl.rearrange("p (a b) -> p a b", b=16),
                                axis=AX, op=Op.max,
                                apply_absolute_value=True).then_inc(semh["rb"], 1)
                e.wait_ge(semh["rb"], nrb)
            P.on("vector", v_wamax)
            rb_after.append(P.val("rb"))
            gti += 1

        def v_gmax(e, wi=wi, nrb=P.inc("rb", 1)):
            e.tensor_reduce(out=L["wred"][:], in_=wamax[:, wi * 2048:(wi + 1) * 2048],
                            axis=AX, op=Op.max).then_inc(semh["rb"], 1)
            e.wait_ge(semh["rb"], nrb)
            e.nop().then_inc(semh["lv"], 1)
        P.on("vector", v_gmax)
        n_lv = P.inc("lv", 1)

        def t_gmax(e, n=n_lv):
            e.wait_ge(semh["lv"], n)
            e.transpose(out=bank[7][0:1, 0:128], in_=L["wred"][:, 0:1],
                        identity=L["c_ident"][:, :]).then_inc(semh["tb"], 1)
        P.on("tensor", t_gmax)
        n_t = P.inc("tb", 1)

        def v_gmax2(e, wi=wi, n=n_t, nrb=P.inc("rb", 1)):
            e.wait_ge(semh["tb"], n)
            e.tensor_reduce(out=stmp[0:1, 13 + wi:14 + wi], in_=bank[7][0:1, 0:128],
                            axis=AX, op=Op.max).then_inc(semh["rb"], 1)
            e.wait_ge(semh["rb"], nrb)
        P.on("vector", v_gmax2)

    def v_scal(e, nrb=P.inc("rb", 6)):
        for wi in range(3):
            e.reciprocal(out=stmp[0:1, 16 + wi:17 + wi],
                         in_=stmp[0:1, 13 + wi:14 + wi]).then_inc(semh["rb"], 1)
        e.wait_ge(semh["rb"], nrb - 3)
        for wi in range(3):
            e.tensor_scalar(out=stmp[0:1, 19 + wi:20 + wi], in0=stmp[0:1, 16 + wi:17 + wi],
                            scalar1=2688.0, scalar2=None, op0=Op.mult)
        li = e.nop().then_inc(semh["rb"], 1)
        e.wait_ge(semh["rb"], nrb - 2)
        for wi, (c6, c2) in enumerate([(7, 8), (9, 10), (11, 12)]):
            e.tensor_scalar(out=stmp[0:1, c6:c6 + 1], in0=stmp[0:1, 19 + wi:20 + wi],
                            scalar1=1.0 / 6.0, scalar2=None, op0=Op.mult)
            e.tensor_scalar(out=stmp[0:1, c2:c2 + 1], in0=stmp[0:1, 19 + wi:20 + wi],
                            scalar1=2.0, scalar2=None, op0=Op.mult)
        e.tensor_tensor(out=stmp[0:1, 22:23], in0=stmp[0:1, 19:20],
                        in1=L["c_sx"][0:1, 0:1], op=Op.mult)
        e.tensor_tensor(out=stmp[0:1, 23:24], in0=stmp[0:1, 20:21],
                        in1=L["c_sx"][0:1, 0:1], op=Op.mult)
        e.tensor_tensor(out=stmp[0:1, 24:25], in0=stmp[0:1, 21:22],
                        in1=L["c_sxd"][0:1, 0:1], op=Op.mult)
        e.nop().then_inc(semh["rb"], 1)
        e.wait_ge(semh["rb"], nrb - 1)
        e.reciprocal(out=stmp[0:1, 4:5], in_=stmp[0:1, 22:23])
        e.reciprocal(out=stmp[0:1, 5:6], in_=stmp[0:1, 23:24])
        e.reciprocal(out=stmp[0:1, 6:7], in_=stmp[0:1, 24:25]).then_inc(semh["rb"], 1)
        e.wait_ge(semh["rb"], nrb)
        e.tensor_tensor(out=stmp[0:1, 25:26], in0=stmp[0:1, 5:6],
                        in1=L["c_sxd"][0:1, 0:1], op=Op.mult)
        e.nop().then_inc(semh["rb"], 1)
        e.wait_ge(semh["rb"], nrb + 1)
        e.tensor_scalar(out=stmp[0:1, 2:3], in0=stmp[0:1, 25:26], scalar1=1.0 / 6.0,
                        scalar2=None, op0=Op.mult)
        e.tensor_scalar(out=stmp[0:1, 3:4], in0=stmp[0:1, 25:26], scalar1=2.0,
                        scalar2=None, op0=Op.mult)
        e.nop().then_inc(semh["lv"], 1)
    P.on("vector", v_scal)
    P.inc("rb", 1)
    n_lv2 = P.inc("lv", 1)

    def t_pp2(e, n=n_lv2):
        e.wait_ge(semh["lv"], n)
        e.matmul(out=bank[7][:, 0:16], lhsT=L["c_ones"][0:1, 0:128],
                 rhs=stmp[0:1, 0:16], start=True, stop=True).then_inc(semh["tb"], 1)
    P.on("tensor", t_pp2)
    n_tpp = P.inc("tb", 1)

    def v_pp2(e, n=n_tpp):
        e.wait_ge(semh["tb"], n)
        e.tensor_copy(out=PP[:], in_=bank[7][:, 0:16]).then_inc(semh["lv"], 1)
    P.on("vector", v_pp2)
    n_pp2 = P.inc("lv", 1)

    gti = 0
    qv_after = []
    for wi, (wn, qn, ntile, FDW, c6, c2) in enumerate(W_CFG):
        nbw = FDW // 16
        for ti in range(ntile):
            sl = wtile[:, (gti % 2) * 4096:(gti % 2) * 4096 + FDW]

            def s_wload2(e, wn=wn, ti=ti, sl=sl,
                         wq=(qv_after[gti - 2] if gti >= 2 else None),
                         first=(gti == 0), npp=n_pp2, n4=n_ph4_done):
                if first:
                    e.wait_ge(semh["lv"], npp)
                    e.wait_ge(semh["sout"], n4)
                if wq is not None:
                    e.wait_ge(semh["qv"], wq)
                e.dma_start(out=sl, in_=ap[wn][ti * 128:(ti + 1) * 128, :]).then_inc(semh["sw"], 16)
            P.on("sync", s_wload2)
            n_l = P.inc("sw", 16)

            def v_wq_pre(e, n=n_l, wscr=P.val("gy")):
                e.wait_ge(semh["sw"], n)
                e.wait_ge(semh["gy"], wscr)
            P.on("vector", v_wq_pre)
            for q in range(FDW // 1024):
                emit_quant_v(P, semh, sl[:, q * 1024:(q + 1) * 1024], 1024, qt,
                             PP[:, c6:c6 + 1], PP[:, c2:c2 + 1],
                             wvq[:, q * 1024:(q + 1) * 1024],
                             amax_src=wamax[:, wi * 2048 + ti * nbw + q * 64:
                                            wi * 2048 + ti * nbw + (q + 1) * 64])
            n_q = P.val("qv")
            def s_wst(e, qn=qn, ti=ti, FDW=FDW, n=n_q):
                e.wait_ge(semh["qv"], n)
                e.dma_start(out=ap[qn][ti * 128:(ti + 1) * 128, :],
                            in_=wvq[:, 0:FDW]).then_inc(semh["gy"], 16)
            P.on("scalar", s_wst)
            P.inc("gy", 16)
            qv_after.append(P.val("qv"))
            gti += 1

    hstage = L["hstage"]; sgt = L["sgt"]; gwb = L["gwb"]; uwb = L["uwb"]; dwb = L["dwb"]
    xat, aat = L["xat"], L["aat"]
    vqh_qv = L["out_t"].ap().bitcast(BF16)
    n_wq_done = P.val("gy")

    for su in range(NSUP):
        for it in range(8):
            for c in range(NC):
                def s_gw(e, su=su, it=it, c=c, tprev=P.val("tb"),
                         first=(su == 0 and it == 0 and c == 0), nw=n_wq_done):
                    if first:
                        e.wait_ge(semh["gy"], nw)
                    e.wait_ge(semh["tb"], max(tprev - 1, 0))
                    e.dma_start(out=gwb[:, (c % 2) * 512:(c % 2 + 1) * 512],
                                in_=ap["wq_g"][it * 512:(it + 1) * 512, c * 128:(c + 1) * 128],
                                transpose=True).then_inc(semh["sw"], 16)
                    e.dma_start(out=uwb[:, (c % 2) * 512:(c % 2 + 1) * 512],
                                in_=ap["wq_u"][it * 512:(it + 1) * 512, c * 128:(c + 1) * 128],
                                transpose=True).then_inc(semh["sw"], 16)
                P.on("sync", s_gw)
                n_w = P.inc("sw", 32)

                def t_gu(e, su=su, it=it, c=c, n=n_w, vprev=P.val("f5"),
                         first=(su == 0 and it == 0 and c == 0), npp=n_pp2,
                         mprev=P.val("m4")):
                    e.wait_ge(semh["sw"], n)
                    if first:
                        e.wait_ge(semh["lv"], npp)
                    if c == 0 and not (su == 0 and it == 0):
                        e.wait_ge(semh["f5"], vprev)
                    if it == 0 and c == 0 and su > 0:
                        e.wait_ge(semh["m4"], mprev)
                    for t in range(4):
                        e.matmul(out=bank[t][:, 0:512],
                                 lhsT=vqxnT[:, c * T + (su * 4 + t) * 128: c * T + (su * 4 + t + 1) * 128],
                                 rhs=gwb[:, (c % 2) * 512:(c % 2 + 1) * 512],
                                 start=(c == 0), stop=(c == NC - 1))
                    for t in range(4):
                        li = e.matmul(out=bank[4 + t][:, 0:512],
                                      lhsT=vqxnT[:, c * T + (su * 4 + t) * 128: c * T + (su * 4 + t + 1) * 128],
                                      rhs=uwb[:, (c % 2) * 512:(c % 2 + 1) * 512],
                                      start=(c == 0), stop=(c == NC - 1))
                    li.then_inc(semh["tb"], 1)
                P.on("tensor", t_gu)
                P.inc("tb", 1)

            n_tc = P.val("tb")
            for t in range(4):
                def s_silu(e, t=t, n=n_tc, vprev=P.val("f5")):
                    e.wait_ge(semh["tb"], n)
                    if t >= 2:
                        e.wait_ge(semh["f5"], vprev - 1)
                    e.activation(out=sgt[:, (t % 2) * 512:(t % 2 + 1) * 512],
                                 in_=bank[t][:, 0:512], func=AF.Silu, bias=0.0,
                                 scale=PP[:, 4:5]).then_inc(semh["ls"], 1)
                P.on("scalar", s_silu)
                n_s = P.inc("ls", 1)

                def v_h(e, it=it, t=t, n=n_s):
                    e.wait_ge(semh["ls"], n)
                    e.tensor_tensor(out=hstage[t][:, (it % 2) * 512:(it % 2 + 1) * 512],
                                    in0=sgt[:, (t % 2) * 512:(t % 2 + 1) * 512],
                                    in1=bank[4 + t][:, 0:512], op=Op.mult).then_inc(semh["f5"], 1)
                P.on("vector", v_h)
                P.inc("f5", 1)

            if it % 2 == 1:
                pair = it // 2
                for t in range(4):
                    def v_q_pre(e, wscr=P.val("gy"), ngo=P.val("go")):
                        e.wait_ge(semh["gy"], wscr)
                        if ngo > 0:
                            e.wait_ge(semh["go"], ngo)
                    P.on("vector", v_q_pre)
                    n_q = emit_quant_v(P, semh, hstage[t].ap(), 1024, qt,
                                       PP[:, 2:3], PP[:, 3:4], vqh_qv[:, 0:1024])

                    def s_vq(e, t=t, pair=pair, n=n_q):
                        e.wait_ge(semh["qv"], n)
                        e.dma_start(out=ap["vqh_scr"][t * 128:(t + 1) * 128,
                                                      pair * 1024:(pair + 1) * 1024],
                                    in_=vqh_qv[:, 0:1024]).then_inc(semh["gy"], 16)
                    P.on("scalar", s_vq)
                    P.inc("gy", 16)

        n_vqh = P.val("gy")

        def s_vqhT(e, n=n_vqh):
            e.wait_ge(semh["gy"], n)
            for Kc in range(32):
                e.dma_start(out=vqhT[:, Kc * 512:(Kc + 1) * 512],
                            in_=ap["vqh_scr"][:, Kc * 128:(Kc + 1) * 128],
                            transpose=True).then_inc(semh["str1"], 16)
        P.on("sync", s_vqhT)
        n_vqhT = P.inc("str1", 16 * 32)

        sout0 = P.val("go")
        for o2 in range(2):
            for Kc in range(32):
                def s_dw(e, o2=o2, Kc=Kc, tprev=P.val("tb")):
                    e.wait_ge(semh["tb"], max(tprev - 1, 0))
                    e.dma_start(out=dwb[:, (Kc % 2) * 512:(Kc % 2 + 1) * 512],
                                in_=ap["wq_d"][o2 * 512:(o2 + 1) * 512, Kc * 128:(Kc + 1) * 128],
                                transpose=True).then_inc(semh["sw"], 16)
                P.on("sync", s_dw)
                n_w = P.inc("sw", 16)

                def t_d(e, o2=o2, Kc=Kc, su=su, n=n_w, nT=n_vqhT, vprev=P.val("m4")):
                    e.wait_ge(semh["sw"], n)
                    if Kc == 0:
                        e.wait_ge(semh["str1"], nT)
                        if not (su == 0 and o2 == 0):
                            e.wait_ge(semh["m4"], vprev)
                    for t in range(4):
                        li = e.matmul(out=bank[t][:, 0:512],
                                      lhsT=vqhT[:, Kc * 512 + t * 128: Kc * 512 + (t + 1) * 128],
                                      rhs=dwb[:, (Kc % 2) * 512:(Kc % 2 + 1) * 512],
                                      start=(Kc == 0), stop=(Kc == 31))
                    li.then_inc(semh["tb"], 1)
                P.on("tensor", t_d)
                P.inc("tb", 1)

            n_td = P.val("tb")
            for t in range(4):
                tok = su * 4 + t
                idx = o2 * 4 + t

                def s_xld(e, tok=tok, t=t, o2=o2, mprev=P.val("m4")):
                    if o2 * 4 + t >= 2:
                        e.wait_ge(semh["m4"], mprev - 1)
                    e.dma_start(out=xat[:, (t % 2) * 512:(t % 2 + 1) * 512],
                                in_=ap["x"][tok * 128:(tok + 1) * 128,
                                            o2 * 512:(o2 + 1) * 512]).then_inc(semh["sxa"], 16)
                    e.dma_start(out=aat[:, (t % 2) * 512:(t % 2 + 1) * 512],
                                in_=ap["attn_scr"][tok * 128:(tok + 1) * 128,
                                                   o2 * 512:(o2 + 1) * 512]).then_inc(semh["sxa"], 16)
                P.on("scalar", s_xld)
                n_x = P.inc("sxa", 32)

                def v_ev(e, t=t, idx=idx, tok=tok, o2=o2, n=n_td, nx=n_x,
                         s0=sout0, nrb=P.inc("rb", 1)):
                    e.wait_ge(semh["tb"], n)
                    e.wait_ge(semh["sxa"], nx)
                    ngo = s0 + 16 * max(idx - 1, 0)
                    if ngo > 0:
                        e.wait_ge(semh["go"], ngo)
                    e.scalar_tensor_tensor(out=qt["tm"][:, (t % 2) * 512:(t % 2 + 1) * 512],
                                           in0=bank[t][:, 0:512],
                                           scalar=PP[:, 6:7],
                                           in1=aat[:, (t % 2) * 512:(t % 2 + 1) * 512],
                                           op0=Op.mult, op1=Op.add).then_inc(semh["rb"], 1)
                    e.wait_ge(semh["rb"], nrb)
                    e.tensor_tensor(out=qt["at"][:, (t % 2) * 512:(t % 2 + 1) * 512],
                                    in0=qt["tm"][:, (t % 2) * 512:(t % 2 + 1) * 512],
                                    in1=xat[:, (t % 2) * 512:(t % 2 + 1) * 512],
                                    op=Op.add).then_inc(semh["m4"], 1)
                P.on("vector", v_ev)
                n_e = P.inc("m4", 1)

                def s_out(e, tok=tok, t=t, o2=o2, n=n_e):
                    e.wait_ge(semh["m4"], n)
                    e.dma_start(out=ap["y"][tok * 128:(tok + 1) * 128, o2 * 512:(o2 + 1) * 512],
                                in_=qt["at"][:, (t % 2) * 512:(t % 2 + 1) * 512]).then_inc(semh["go"], 16)
                P.on("scalar", s_out)
                P.inc("go", 16)

    if "dbg_pp" in ap:
        scr_b = P.val("gq")
        def g_pp(e, n=P.val("go"), sb2=scr_b):
            e.wait_ge(semh["go"], n)
            e.dma_start(out=ap["dbg_pp"][:, 0:16], in_=PP[:]).then_inc(semh["gq"], 16)
            e.dma_start(out=ap["dbg_pp"][0:1, 16:32], in_=stmp[0:1, 0:16]).then_inc(semh["gq"], 16)
            e.dma_start(out=ap["dbg_pp"][:, 32:48],
                        in_=wamax[:, 0:16]).then_inc(semh["gq"], 16)
            e.wait_ge(semh["gq"], sb2 + 48)
        P.on("gpsimd", g_pp)
        P.inc("gq", 48)

    def g_fin(e, n=P.val("go"), ns=P.val("sout")):
        e.wait_ge(semh["sout"], ns)
        e.wait_ge(semh["go"], n)
    P.on("gpsimd", g_fin)


def kernel(**inputs):
    import ml_dtypes as _ml
    lay = build_layouts(inputs)
    x_all = np.asarray(inputs["L0_attn_inp"], dtype=np.float32)
    gate_w = np.asarray(inputs["L0_ffn_gate_w"], dtype=np.float32)
    up_w = np.asarray(inputs["L0_ffn_up_w"], dtype=np.float32)
    down_w = np.asarray(inputs["L0_ffn_down_w"], dtype=np.float32)
    s_in = np.asarray(inputs["L0_ffn_s_in"], dtype=np.float32).reshape(1, 1)
    s_in_down = np.asarray(inputs["L0_ffn_s_in_down"], dtype=np.float32).reshape(1, 1)
    identb = np.eye(128).astype(_ml.bfloat16)

    common = dict(gate_w=gate_w, up_w=up_w, down_w=down_w,
                  s_in=s_in, s_in_down=s_in_down, identb=identb, **lay)
    in_maps = [dict(x=np.ascontiguousarray(x_all[b]), **common) for b in range(8)]

    nc = build_core("all")
    res = run_bass_kernel_spmd(nc, in_maps, core_ids=list(range(8)))
    out = np.stack([np.asarray(res.results[b]["y"], dtype=np.float32) for b in range(8)])
    return out


# version 02
# speedup vs baseline: 1.3315x; 1.3315x over previous
import numpy as np
from contextlib import ExitStack

import concourse.bass as bass
import concourse.mybir as mybir
from concourse.bass_utils import run_bass_kernel_spmd
from concourse.alu_op_type import AluOpType as Op

F32 = mybir.dt.float32
I32 = mybir.dt.int32
BF16 = mybir.dt.bfloat16
AF = mybir.ActivationFunctionType
AX = mybir.AxisListType.X

T = 2048
D = 1024
H = 16
CH = 64
NST = 64
I_FF = 4096
NT = T // 128
NC = D // 128
TSUP = 512
NSUP = T // TSUP

EXP_BITS = 0x7F800000
C_REC8 = float(0x80800000)
C_PM3 = float(-(3 << 23))


def _sem_counter():
    return {"v": 0}


class Prog:

    def __init__(self, nc):
        self.nc = nc
        self.q = {"sync": [], "scalar": [], "vector": [], "tensor": [], "gpsimd": []}
        self.cnt = {}

    def on(self, eng, fn):
        self.q[eng].append(fn)

    def inc(self, sem_name, by):
        self.cnt[sem_name] = self.cnt.get(sem_name, 0) + by
        return self.cnt[sem_name]

    def val(self, sem_name):
        return self.cnt.get(sem_name, 0)


def build_layouts(inputs):
    W_XBC = np.asarray(inputs["L0_attn_W_XBC"])
    b_XBC = np.asarray(inputs["L0_attn_b_XBC"])
    W_gate = np.asarray(inputs["L0_attn_W_gate"])
    b_gate = np.asarray(inputs["L0_attn_b_gate"])
    W_logA = np.asarray(inputs["L0_attn_W_logA"])
    b_logA = np.asarray(inputs["L0_attn_b_logA"])
    W_out = np.asarray(inputs["L0_attn_W_out"])
    b_out = np.asarray(inputs["L0_attn_b_out"])

    wxbcT = np.zeros((128, NC * 192), np.float32)
    wgT = np.zeros((128, NC * 64), np.float32)
    wlogAT = np.zeros((128, NC * 16), np.float32)
    bBs = np.zeros((128, 16), np.float32)
    bCs = np.zeros((128, 16), np.float32)
    bXs = np.zeros((128, 16), np.float32)
    for h in range(H):
        c = h // 2
        po = 64 * (h % 2)
        wxbcT[po:po + 64, c * 192:(c + 1) * 192] = W_XBC[h].T
        wgT[po:po + 64, c * 64:(c + 1) * 64] = W_gate[h].T
        bXs[0:64, h] = b_XBC[h, 0:64]
        bBs[0:64, h] = b_XBC[h, 64:128]
        bCs[0:64, h] = b_XBC[h, 128:192]
    for h in range(H):
        for c in range(NC):
            rows = slice(c * 128, (c + 1) * 128)
            cols = slice(c * 16, (c + 1) * 16)
    wlogA_full = np.zeros((D, H), np.float32)
    for h in range(H):
        wlogA_full[64 * h:64 * (h + 1), h] = W_logA[h, 0, :]
    for c in range(NC):
        wlogAT[:, c * 16:(c + 1) * 16] = wlogA_full[c * 128:(c + 1) * 128, :]

    bgrep = np.tile(b_gate.reshape(1, H * 64), (128, 1)).astype(np.float32)
    woT = np.zeros((128, NC * D), np.float32)
    for c in range(NC):
        woT[:, c * D:(c + 1) * D] = W_out.T[c * 128:(c + 1) * 128, :]
    borep = np.tile(b_out.reshape(1, D), (128, 1)).astype(np.float32)

    ident = np.eye(128, dtype=np.float32)
    sel16 = np.repeat(np.eye(16, dtype=np.float32), 128, axis=1)
    import ml_dtypes as _ml
    z_16x128 = np.zeros((16, 128), np.float32)
    z_64f = np.zeros((64, 1024), np.float32)
    z_64b = np.zeros((64, 1024), np.float32).astype(_ml.bfloat16)
    z_stmp = np.zeros((1, 32), np.float32)
    epsrep = np.full((128, 1), 1e-5, np.float32)
    tri = np.triu(np.ones((128, 128), np.float32))
    ones128 = np.ones((128, 128), np.float32)
    blogA = b_logA.astype(np.float32).reshape(16, 1)

    return dict(wxbcT=wxbcT, wgT=wgT, wlogAT=wlogAT, bBs=bBs, bCs=bCs, bXs=bXs,
                bgrep=bgrep, woT=woT, borep=borep, ident=ident, tri=tri,
                ones128=ones128, blogA=blogA, sel16=sel16,
                z_16x128=z_16x128, z_64f=z_64f, z_64b=z_64b, z_stmp=z_stmp,
                epsrep=epsrep)


def emit_quant(P, name, src_ap, FD, tiles, gs6_pp, gs2_pp, vq_ap, sem, wait_list=None):
    nb = FD // 16
    amax = tiles["amax"]; scb = tiles["scb"]; ebi = tiles["ebi"]; ebf = tiles["ebf"]
    r8i = tiles["r8i"]; m8 = tiles["m8"]; n8a = tiles["n8a"]; n8b = tiles["n8b"]
    p8i = tiles["p8i"]; sc8 = tiles["sc8"]; rcp = tiles["rcp"]
    rcp2 = tiles["rcp2"]; sc8h = tiles["sc8h"]
    tm = tiles["tm"]; at = tiles["at"]; qa = tiles["qa"]; qb = tiles["qb"]
    ga = tiles["ga"]; gb = tiles["gb"]; sgn = tiles["sgn"]

    qv = "qv_" + name
    qs = "qs_" + name
    qg = "qg_" + name

    def V(fn):
        P.on("vector", fn)

    def S(fn):
        P.on("scalar", fn)

    def G(fn):
        P.on("gpsimd", fn)

    sem_h = sem

    def v_part1(e, src_ap=src_ap):
        e.tensor_reduce(out=amax[:, 0:nb], in_=src_ap.rearrange("p (a b) -> p a b", b=16),
                        axis=AX, op=Op.max, apply_absolute_value=True)
        e.tensor_scalar(out=scb[:, 0:nb], in0=amax[:, 0:nb], scalar1=gs6_pp,
                        scalar2=448.0, op0=Op.mult, op1=Op.min)
        e.tensor_scalar(out=ebi[:, 0:nb], in0=scb[:, 0:nb].bitcast(I32),
                        scalar1=EXP_BITS, scalar2=None, op0=Op.bitwise_and)
        e.tensor_scalar(out=ebf[:, 0:nb], in0=ebi[:, 0:nb].bitcast(F32),
                        scalar1=float(2.0 ** -6), scalar2=None, op0=Op.max)
        e.tensor_scalar(out=r8i[:, 0:nb], in0=ebf[:, 0:nb].bitcast(I32),
                        scalar1=-1.0, scalar2=C_REC8, op0=Op.mult, op1=Op.add)
        e.tensor_scalar(out=p8i[:, 0:nb], in0=ebf[:, 0:nb].bitcast(I32),
                        scalar1=C_PM3, scalar2=None, op0=Op.add)
        e.tensor_tensor(out=m8[:, 0:nb], in0=scb[:, 0:nb], in1=r8i[:, 0:nb].bitcast(F32),
                        op=Op.mult)
        e.tensor_scalar(out=n8a[:, 0:nb], in0=m8[:, 0:nb], scalar1=8.5, scalar2=None,
                        op0=Op.is_ge)
        for k in range(1, 8):
            s_, d_ = (n8a, n8b) if k % 2 == 1 else (n8b, n8a)
            e.scalar_tensor_tensor(out=d_[:, 0:nb], in0=m8[:, 0:nb], scalar=8.5 + k,
                                   in1=s_[:, 0:nb], op0=Op.is_ge, op1=Op.add)
        e.scalar_tensor_tensor(out=sc8[:, 0:nb], in0=n8b[:, 0:nb], scalar=8.0,
                               in1=p8i[:, 0:nb].bitcast(F32), op0=Op.add, op1=Op.mult)
        e.reciprocal(out=rcp[:, 0:nb], in_=sc8[:, 0:nb]).then_inc(sem_h["rb"], 1)
        e.tensor_scalar(out=sc8h[:, 0:nb], in0=sc8[:, 0:nb], scalar1=0.5,
                        scalar2=None, op0=Op.mult)
        e.wait_ge(sem_h["rb"], nrb)
        e.tensor_scalar(out=rcp2[:, 0:nb], in0=rcp[:, 0:nb], scalar1=gs2_pp,
                        scalar2=None, op0=Op.mult)
        e.tensor_tensor(out=tm[:, 0:FD].rearrange("p (a b) -> p a b", b=16),
                        in0=src_ap.rearrange("p (a b) -> p a b", b=16),
                        in1=rcp2[:, 0:nb].unsqueeze(-1).broadcast_to([128, nb, 16]),
                        op=Op.mult).then_inc(sem_h[qv], 1)
    V(v_part1)
    n_qv1 = P.inc(qv, 1)

    def s_part(e, src_ap=src_ap, n=n_qv1):
        e.wait_ge(sem_h[qv], n)
        e.activation(out=at[:, 0:FD], in_=tm[:, 0:FD], func=AF.Abs)
        e.activation(out=sgn[:, 0:FD], in_=src_ap, func=AF.Sign).then_inc(sem_h[qs], 1)
    S(s_part)
    n_qs = P.inc(qs, 1)

    def g_part(e, n=n_qs):
        e.wait_ge(sem_h[qs], n)
        e.tensor_scalar(out=ga[:, 0:FD], in0=at[:, 0:FD], scalar1=10.0, scalar2=2.0,
                        op0=Op.is_ge, op1=Op.mult)
        e.scalar_tensor_tensor(out=gb[:, 0:FD], in0=at[:, 0:FD], scalar=7.0,
                               in1=ga[:, 0:FD], op0=Op.is_ge, op1=Op.add)
        e.scalar_tensor_tensor(out=ga[:, 0:FD], in0=at[:, 0:FD], scalar=5.0,
                               in1=gb[:, 0:FD], op0=Op.is_ge, op1=Op.add).then_inc(sem_h[qg], 1)
    G(g_part)
    n_qg = P.inc(qg, 1)

    def v_part2(e, ns=n_qs, ng=n_qg):
        e.wait_ge(sem_h[qs], ns)
        e.tensor_scalar(out=qa[:, 0:FD], in0=at[:, 0:FD], scalar1=0.5, scalar2=None,
                        op0=Op.is_ge)
        for i, thr in enumerate((1.5, 2.5, 3.5)):
            s_, d_ = (qa, qb) if i % 2 == 0 else (qb, qa)
            e.scalar_tensor_tensor(out=d_[:, 0:FD], in0=at[:, 0:FD], scalar=thr,
                                   in1=s_[:, 0:FD], op0=Op.is_ge, op1=Op.add)
        e.wait_ge(sem_h[qg], ng)
        e.scalar_tensor_tensor(out=qa[:, 0:FD], in0=ga[:, 0:FD], scalar=2.0,
                               in1=qb[:, 0:FD], op0=Op.mult, op1=Op.add)
        e.tensor_tensor(out=qb[:, 0:FD], in0=qa[:, 0:FD], in1=sgn[:, 0:FD], op=Op.mult)
        e.tensor_tensor(out=vq_ap.rearrange("p (a b) -> p a b", b=16),
                        in0=qb[:, 0:FD].rearrange("p (a b) -> p a b", b=16),
                        in1=sc8h[:, 0:nb].unsqueeze(-1).broadcast_to([128, nb, 16]),
                        op=Op.mult).then_inc(sem_h[qv], 1)
    V(v_part2)
    P.inc(qv, 1)
    if wait_list is not None:
        wait_list.append((qv, P.val(qv)))
    return P.val(qv)


def emit_quant_v(P, sem_h, src_ap, FD, tiles, gs6_pp, gs2_pp, vq_ap,
                 amax_src=None):
    nb = FD // 16
    amax = tiles["amax"]; scb = tiles["scb"]; ebi = tiles["ebi"]; ebf = tiles["ebf"]
    r8i = tiles["r8i"]; m8 = tiles["m8"]; n8a = tiles["n8a"]; n8b = tiles["n8b"]
    p8i = tiles["p8i"]; sc8 = tiles["sc8"]; rcp = tiles["rcp"]
    rcp2 = tiles["rcp2"]; sc8h = tiles["sc8h"]
    tm = tiles["tm"]; at = tiles["at"]; qa = tiles["qa"]; qb = tiles["qb"]
    ga = tiles["ga"]; gb = tiles["gb"]

    nrb_am = P.inc("rb", 2)

    def v_part1(e, src_ap=src_ap, amax_src=amax_src, nrb=nrb_am):
        if amax_src is None:
            e.tensor_reduce(out=amax[:, 0:nb],
                            in_=src_ap.rearrange("p (a b) -> p a b", b=16),
                            axis=AX, op=Op.max,
                            apply_absolute_value=True).then_inc(sem_h["rb"], 1)
            e.wait_ge(sem_h["rb"], nrb - 1)
            am = amax[:, 0:nb]
        else:
            e.nop().then_inc(sem_h["rb"], 1)
            am = amax_src
        e.tensor_scalar(out=scb[:, 0:nb], in0=am, scalar1=gs6_pp,
                        scalar2=448.0, op0=Op.mult, op1=Op.min)
        e.tensor_scalar(out=ebi[:, 0:nb], in0=scb[:, 0:nb].bitcast(I32),
                        scalar1=EXP_BITS, scalar2=None, op0=Op.bitwise_and)
        e.tensor_scalar(out=ebf[:, 0:nb], in0=ebi[:, 0:nb].bitcast(F32),
                        scalar1=float(2.0 ** -6), scalar2=None, op0=Op.max)
        e.tensor_scalar(out=r8i[:, 0:nb], in0=ebf[:, 0:nb].bitcast(I32),
                        scalar1=-1.0, scalar2=C_REC8, op0=Op.mult, op1=Op.add)
        e.tensor_scalar(out=p8i[:, 0:nb], in0=ebf[:, 0:nb].bitcast(I32),
                        scalar1=C_PM3, scalar2=None, op0=Op.add)
        e.tensor_tensor(out=m8[:, 0:nb], in0=scb[:, 0:nb],
                        in1=r8i[:, 0:nb].bitcast(F32), op=Op.mult)
        e.tensor_scalar(out=n8a[:, 0:nb], in0=m8[:, 0:nb], scalar1=8.5,
                        scalar2=None, op0=Op.is_ge)
        for k in range(1, 8):
            s_, d_ = (n8a, n8b) if k % 2 == 1 else (n8b, n8a)
            e.scalar_tensor_tensor(out=d_[:, 0:nb], in0=m8[:, 0:nb], scalar=8.5 + k,
                                   in1=s_[:, 0:nb], op0=Op.is_ge, op1=Op.add)
        e.scalar_tensor_tensor(out=sc8[:, 0:nb], in0=n8b[:, 0:nb], scalar=8.0,
                               in1=p8i[:, 0:nb].bitcast(F32), op0=Op.add, op1=Op.mult)
        e.reciprocal(out=rcp[:, 0:nb], in_=sc8[:, 0:nb]).then_inc(sem_h["rb"], 1)
        e.tensor_scalar(out=sc8h[:, 0:nb], in0=sc8[:, 0:nb], scalar1=0.5,
                        scalar2=None, op0=Op.mult)
        e.wait_ge(sem_h["rb"], nrb)
        e.tensor_scalar(out=rcp2[:, 0:nb], in0=rcp[:, 0:nb], scalar1=gs2_pp,
                        scalar2=None, op0=Op.mult)
        e.tensor_tensor(out=tm[:, 0:FD].rearrange("p (a b) -> p a b", b=16),
                        in0=src_ap.rearrange("p (a b) -> p a b", b=16),
                        in1=rcp2[:, 0:nb].unsqueeze(-1).broadcast_to([128, nb, 16]),
                        op=Op.mult).then_inc(sem_h["qv"], 1)
    P.on("vector", v_part1)
    n_qv1 = P.inc("qv", 1)

    def s_part(e, src_ap=src_ap, n=n_qv1):
        e.wait_ge(sem_h["qv"], n)
        e.activation(out=at[:, 0:FD], in_=tm[:, 0:FD], func=AF.Abs)
        e.activation(out=tm[:, 0:FD], in_=src_ap, func=AF.Sign).then_inc(sem_h["qs"], 1)
    P.on("scalar", s_part)
    n_qs = P.inc("qs", 1)

    def v_part2(e, ns=n_qs, vq_ap=vq_ap):
        e.wait_ge(sem_h["qs"], ns)
        e.tensor_scalar(out=ga[:, 0:FD], in0=at[:, 0:FD], scalar1=10.0, scalar2=2.0,
                        op0=Op.is_ge, op1=Op.mult)
        e.scalar_tensor_tensor(out=gb[:, 0:FD], in0=at[:, 0:FD], scalar=7.0,
                               in1=ga[:, 0:FD], op0=Op.is_ge, op1=Op.add)
        e.scalar_tensor_tensor(out=ga[:, 0:FD], in0=at[:, 0:FD], scalar=5.0,
                               in1=gb[:, 0:FD], op0=Op.is_ge, op1=Op.add)
        e.tensor_scalar(out=qa[:, 0:FD], in0=at[:, 0:FD], scalar1=0.5, scalar2=None,
                        op0=Op.is_ge)
        for i, thr in enumerate((1.5, 2.5, 3.5)):
            s_, d_ = (qa, qb) if i % 2 == 0 else (qb, qa)
            e.scalar_tensor_tensor(out=d_[:, 0:FD], in0=at[:, 0:FD], scalar=thr,
                                   in1=s_[:, 0:FD], op0=Op.is_ge, op1=Op.add)
        e.scalar_tensor_tensor(out=qa[:, 0:FD], in0=ga[:, 0:FD], scalar=2.0,
                               in1=qb[:, 0:FD], op0=Op.mult, op1=Op.add)
        e.tensor_tensor(out=qb[:, 0:FD], in0=qa[:, 0:FD], in1=tm[:, 0:FD], op=Op.mult)
        e.tensor_tensor(out=vq_ap.rearrange("p (a b) -> p a b", b=16),
                        in0=qb[:, 0:FD].rearrange("p (a b) -> p a b", b=16),
                        in1=sc8h[:, 0:nb].unsqueeze(-1).broadcast_to([128, nb, 16]),
                        op=Op.mult).then_inc(sem_h["qv"], 1)
    P.on("vector", v_part2)
    return P.inc("qv", 1)


W_CHUNKS = []
for _wi, (_wn, _qn, _ntile, _nq, _c6, _c2) in enumerate(
        [("gate_w", "wq_g", 32, 1, 7, 8), ("up_w", "wq_u", 32, 1, 9, 10),
         ("down_w", "wq_d", 8, 4, 11, 12)]):
    for _ti in range(_ntile):
        for _q in range(_nq):
            W_CHUNKS.append((_wn, _qn, _ti, _q, _c6, _c2, _wi))


def emit_quant_vecseg(P, sem_h, src_ap, FD, tiles, gs6_pp, gs2_pp, vq_ap,
                      sw_wait=None):
    nb = FD // 16
    amax = tiles["amax"]; scb = tiles["scb"]; ebi = tiles["ebi"]; ebf = tiles["ebf"]
    r8i = tiles["r8i"]; m8 = tiles["m8"]; n8a = tiles["n8a"]; n8b = tiles["n8b"]
    p8i = tiles["p8i"]; sc8 = tiles["sc8"]; rcp = tiles["rcp"]
    rcp2 = tiles["rcp2"]; sc8h = tiles["sc8h"]
    tm = tiles["tm"]; at = tiles["at"]; qa = tiles["qa"]; qb = tiles["qb"]
    ga = tiles["ga"]; gb = tiles["gb"]

    nrb = P.inc("rb", 2)

    def seg1(e, src_ap=src_ap, nrb=nrb, sw_wait=sw_wait):
        if sw_wait is not None:
            e.wait_ge(sem_h[sw_wait[0]], sw_wait[1])
        e.tensor_reduce(out=amax[:, 0:nb],
                        in_=src_ap.rearrange("p (a b) -> p a b", b=16),
                        axis=AX, op=Op.max,
                        apply_absolute_value=True).then_inc(sem_h["rb"], 1)
        e.wait_ge(sem_h["rb"], nrb - 1)
        e.tensor_scalar(out=scb[:, 0:nb], in0=amax[:, 0:nb], scalar1=gs6_pp,
                        scalar2=448.0, op0=Op.mult, op1=Op.min)
        e.tensor_scalar(out=ebi[:, 0:nb], in0=scb[:, 0:nb].bitcast(I32),
                        scalar1=EXP_BITS, scalar2=None, op0=Op.bitwise_and)
        e.tensor_scalar(out=ebf[:, 0:nb], in0=ebi[:, 0:nb].bitcast(F32),
                        scalar1=float(2.0 ** -6), scalar2=None, op0=Op.max)
        e.tensor_scalar(out=r8i[:, 0:nb], in0=ebf[:, 0:nb].bitcast(I32),
                        scalar1=-1.0, scalar2=C_REC8, op0=Op.mult, op1=Op.add)
        e.tensor_scalar(out=p8i[:, 0:nb], in0=ebf[:, 0:nb].bitcast(I32),
                        scalar1=C_PM3, scalar2=None, op0=Op.add)
        e.tensor_tensor(out=m8[:, 0:nb], in0=scb[:, 0:nb],
                        in1=r8i[:, 0:nb].bitcast(F32), op=Op.mult)
        e.tensor_scalar(out=n8a[:, 0:nb], in0=m8[:, 0:nb], scalar1=8.5,
                        scalar2=None, op0=Op.is_ge)
        for k in range(1, 8):
            s_, d_ = (n8a, n8b) if k % 2 == 1 else (n8b, n8a)
            e.scalar_tensor_tensor(out=d_[:, 0:nb], in0=m8[:, 0:nb], scalar=8.5 + k,
                                   in1=s_[:, 0:nb], op0=Op.is_ge, op1=Op.add)
        e.scalar_tensor_tensor(out=sc8[:, 0:nb], in0=n8b[:, 0:nb], scalar=8.0,
                               in1=p8i[:, 0:nb].bitcast(F32), op0=Op.add, op1=Op.mult)
        e.reciprocal(out=rcp[:, 0:nb], in_=sc8[:, 0:nb]).then_inc(sem_h["rb"], 1)
        e.tensor_scalar(out=sc8h[:, 0:nb], in0=sc8[:, 0:nb], scalar1=0.5,
                        scalar2=None, op0=Op.mult)
        e.wait_ge(sem_h["rb"], nrb)
        e.tensor_scalar(out=rcp2[:, 0:nb], in0=rcp[:, 0:nb], scalar1=gs2_pp,
                        scalar2=None, op0=Op.mult)
        e.tensor_tensor(out=tm[:, 0:FD].rearrange("p (a b) -> p a b", b=16),
                        in0=src_ap.rearrange("p (a b) -> p a b", b=16),
                        in1=rcp2[:, 0:nb].unsqueeze(-1).broadcast_to([128, nb, 16]),
                        op=Op.mult).then_inc(sem_h["qg"], 1)

    def seg2(e):
        e.scalar_tensor_tensor(out=at[:, 0:FD], in0=tm[:, 0:FD], scalar=-1.0,
                               in1=tm[:, 0:FD], op0=Op.mult, op1=Op.max)
        e.tensor_scalar(out=gb[:, 0:FD], in0=tm[:, 0:FD], scalar1=0.0, scalar2=2.0,
                        op0=Op.is_ge, op1=Op.mult)
        e.tensor_scalar(out=ga[:, 0:FD], in0=at[:, 0:FD], scalar1=10.0, scalar2=2.0,
                        op0=Op.is_ge, op1=Op.mult)
        e.scalar_tensor_tensor(out=qb[:, 0:FD], in0=at[:, 0:FD], scalar=7.0,
                               in1=ga[:, 0:FD], op0=Op.is_ge, op1=Op.add)
        e.scalar_tensor_tensor(out=ga[:, 0:FD], in0=at[:, 0:FD], scalar=5.0,
                               in1=qb[:, 0:FD], op0=Op.is_ge, op1=Op.add)

    def seg3(e):
        e.tensor_scalar(out=qa[:, 0:FD], in0=at[:, 0:FD], scalar1=0.5, scalar2=None,
                        op0=Op.is_ge)
        for i, thr in enumerate((1.5, 2.5, 3.5)):
            s_, d_ = (qa, qb) if i % 2 == 0 else (qb, qa)
            e.scalar_tensor_tensor(out=d_[:, 0:FD], in0=at[:, 0:FD], scalar=thr,
                                   in1=s_[:, 0:FD], op0=Op.is_ge, op1=Op.add)

    def seg4(e, vq_ap=vq_ap):
        e.scalar_tensor_tensor(out=qa[:, 0:FD], in0=ga[:, 0:FD], scalar=2.0,
                               in1=qb[:, 0:FD], op0=Op.mult, op1=Op.add)
        e.scalar_tensor_tensor(out=qb[:, 0:FD], in0=gb[:, 0:FD], scalar=-1.0,
                               in1=qa[:, 0:FD], op0=Op.add, op1=Op.mult)
        e.tensor_tensor(out=vq_ap.rearrange("p (a b) -> p a b", b=16),
                        in0=qb[:, 0:FD].rearrange("p (a b) -> p a b", b=16),
                        in1=sc8h[:, 0:nb].unsqueeze(-1).broadcast_to([128, nb, 16]),
                        op=Op.mult).then_inc(sem_h["qv"], 1)

    return [seg1, seg2, seg3, seg4]


class TV:

    def __init__(self, base, p0, np_, c0, cols, dt, base_dt=F32):
        self.base, self.p0, self.np_, self.c0, self.cols = base, p0, np_, c0, cols
        self.dt, self.base_dt = dt, base_dt

    def ap(self):
        a = self.base[self.p0:self.p0 + self.np_, self.c0:self.c0 + self.cols]
        if self.dt != self.base_dt:
            a = a.bitcast(self.dt)
        return a

    def __getitem__(self, sl):
        return self.ap()[sl]


class Arena:
    def __init__(self, base, base_dt=F32):
        self.base = base
        self.base_dt = base_dt
        self.itemsz = {F32: 4, I32: 4, BF16: 2}[base_dt]

    def at(self, c0, np_, free_elems, dt, p0=0):
        itemsz = {F32: 4, I32: 4, BF16: 2}[dt]
        cols = (free_elems * itemsz + self.itemsz - 1) // self.itemsz
        return TV(self.base, p0, np_, c0, cols, dt, self.base_dt)


def build_core(dbg_phase="all"):
    nc = bass.Bass("TRN2", target_bir_lowering=False, detect_race_conditions=False)
    P = Prog(nc)

    ap = {}
    def din(name, shape, dt=F32):
        ap[name] = nc.dram_tensor(name, shape, dt, kind="ExternalInput").ap()
    din("x", [T, D]); din("wxbcT", [128, NC * 192]); din("wgT", [128, NC * 64])
    din("wlogAT", [128, NC * 16]); din("bBs", [128, 16]); din("bCs", [128, 16])
    din("bXs", [128, 16]); din("bgrep", [128, H * 64]); din("woT", [128, NC * D])
    din("borep", [128, D]); din("ident", [128, 128]); din("identb", [128, 128], BF16)
    din("tri", [128, 128]); din("ones128", [128, 128]); din("blogA", [16, 1])
    din("sel16", [16, 16 * 128])
    din("z_16x128", [16, 128]); din("z_64f", [64, 1024]); din("z_64b", [64, 1024], BF16)
    din("z_stmp", [1, 32]); din("epsrep", [128, 1])
    din("gate_w", [I_FF, D]); din("up_w", [I_FF, D]); din("down_w", [D, I_FF])
    din("s_in", [1, 1]); din("s_in_down", [1, 1])

    dbg = dbg_phase != "all"
    def scr(name, shape, dt):
        kind = "ExternalOutput" if dbg else "Internal"
        ap[name] = nc.dram_tensor(name, shape, dt, kind=kind).ap()
    scr("xn_scr", [T, D], BF16); scr("vqxn_scr", [T, D], BF16)
    scr("yg_scr", [T, D], BF16); scr("attn_scr", [T, D], F32)
    scr("vqh_scr", [TSUP, I_FF], BF16)
    scr("wq_g", [D, I_FF], BF16); scr("wq_u", [D, I_FF], BF16)
    scr("wq_d", [I_FF, D], BF16)
    ap["y"] = nc.dram_tensor("y", [T, D], F32, kind="ExternalOutput").ap()
    if dbg:
        ap["dbg_dec"] = nc.dram_tensor("dbg_dec", [16, 2 * T], F32, kind="ExternalOutput").ap()
        ap["dbg_t"] = nc.dram_tensor("dbg_t", [128, 2304], F32, kind="ExternalOutput").ap()
        ap["dbg_pp"] = nc.dram_tensor("dbg_pp", [128, 48], F32, kind="ExternalOutput").ap()

    es = ExitStack()
    with es:
        def sb(name, shape, dt=F32):
            return es.enter_context(nc.sbuf_tensor(name, shape, dt))

        semh = {}
        for s_ in ["sx", "sw", "scr1", "str1", "sout", "qv", "qs", "qg",
                   "lv", "ls", "tb", "vb", "sb_", "m4", "f5", "gq", "rb",
                   "gy", "go", "sxa", "wt"]:
            semh[s_] = es.enter_context(nc.semaphore(s_))

        L = {"semh": semh}

        L["c_wxbcTb"] = sb("c_wxbcTb", [128, NC * 192], BF16)
        L["c_wgTb"] = sb("c_wgTb", [128, NC * 64], BF16)
        L["c_wlogATb"] = sb("c_wlogATb", [128, NC * 16], BF16)
        L["c_bBs"] = sb("c_bBs", [128, 16]); L["c_bCs"] = sb("c_bCs", [128, 16])
        L["c_bXs"] = sb("c_bXs", [128, 16]); L["c_bgrep"] = sb("c_bgrep", [128, H * 64])
        L["c_woTb"] = sb("c_woTb", [128, NC * D], BF16)
        L["c_borep"] = sb("c_borep", [128, D])
        L["c_ident"] = sb("c_ident", [128, 128]); L["c_identb"] = sb("c_identb", [128, 128], BF16)
        L["c_tri"] = sb("c_tri", [128, 128]); L["c_ones"] = sb("c_ones", [128, 128])
        L["c_blogA"] = sb("c_blogA", [16, 1])
        L["c_sx"] = sb("c_sx", [1, 1]); L["c_sxd"] = sb("c_sxd", [1, 1])
        L["PP"] = sb("PP", [128, 16])
        L["stmp"] = sb("stmp", [1, 32])
        L["c_eps"] = sb("c_eps", [128, 1])
        L["gmx"] = sb("gmx", [128, 4])

        QF = 1024
        qt = {}
        for n_, fe, dt in [("amax", QF // 16, F32), ("scb", QF // 16, F32),
                           ("ebi", QF // 16, I32), ("ebf", QF // 16, F32),
                           ("r8i", QF // 16, I32), ("m8", QF // 16, F32),
                           ("n8a", QF // 16, F32), ("n8b", QF // 16, F32),
                           ("p8i", QF // 16, I32), ("sc8", QF // 16, F32),
                           ("rcp", QF // 16, F32), ("rcp2", QF // 16, F32),
                           ("sc8h", QF // 16, F32), ("tm", QF, F32),
                           ("at", QF, F32), ("qa", QF, F32),
                           ("qb", QF, F32), ("ga", QF, F32), ("gb", QF, F32)]:
            qt[n_] = sb("qt_" + n_, [128, fe], dt)
        L["qt"] = qt

        L["logA_sb"] = sb("logA_sb", [16, T]); L["dec_sb"] = sb("dec_sb", [16, T])
        L["ecT_sb"] = sb("ecT_sb", [16, T]); L["zz16"] = sb("zz16", [16, 128])
        L["Bt2"] = sb("Bt2", [64, 256], BF16); L["Ct2"] = sb("Ct2", [64, 256], BF16)
        L["Xt2"] = sb("Xt2", [64, 256], BF16); L["Dt2"] = sb("Dt2", [128, 256], F32)
        L["Ptmp"] = sb("Ptmp", [128, 128], F32); L["Pm2"] = sb("Pm2", [128, 256], BF16)
        L["Xk2"] = sb("Xk2", [128, 192], BF16); L["Bsc2"] = sb("Bsc2", [128, 128], BF16)
        L["btT"] = sb("btT", [128, 128], BF16)
        L["Csc2"] = sb("Csc2", [64, 256], BF16); L["gate_s2"] = sb("gate_s2", [128, 128], F32)
        L["S_sb"] = sb("S_sb", [64, H * 64]); L["Sb_sb"] = sb("Sb_sb", [64, H * 64], BF16)
        L["decColT2"] = sb("decColT2", [128, 32]); L["negdecT2"] = sb("negdecT2", [128, 32])
        L["ebpT2"] = sb("ebpT2", [128, 32]); L["lamrow2"] = sb("lamrow2", [128, 32])
        L["lamT2"] = sb("lamT2", [128, 32])
        L["Stmp"] = sb("Stmp", [64, 64])
        L["diag16"] = sb("diag16", [16, 16])
        L["wred"] = sb("wred", [128, 1])
        L["ln_s"] = sb("ln_s", [128, 8])

        AR_COLS = 11264
        arena = sb("arena", [128, AR_COLS], F32)
        A = Arena(arena, F32)
        ARB_COLS = 32768
        arenab = sb("arenab", [128, ARB_COLS], BF16)
        AB = Arena(arenab, BF16)
        C0 = 0
        L["vqxnT2"] = AB.at(0, 128, 2 * 4096, BF16)
        L["xnT"] = AB.at(8192, 128, NC * T, BF16)
        L["ygT"] = AB.at(8192, 128, NC * T, BF16)
        L["vqhT"] = AB.at(8192, 128, 32 * 512, BF16)
        L["xnb_t"] = AB.at(24576, 128, 2 * D, BF16)
        L["vqxn_t"] = AB.at(26624, 128, 2 * D, BF16)
        L["ygring"] = AB.at(24576, 128, 3 * D, BF16)
        L["wvqb"] = AB.at(27648, 128, 1024, BF16)
        L["wvqT"] = AB.at(28672, 128, 1024, BF16)
        L["gwb"] = AB.at(24576, 128, 4 * 512, BF16)
        L["uwb"] = AB.at(26624, 128, 4 * 512, BF16)
        L["dwb"] = AB.at(28672, 128, 4 * 512, BF16)
        L["vqh_qv"] = AB.at(30720, 128, 2 * 1024, BF16)
        L["st_woT"] = A.at(C0, 128, NC * D // 2, F32)
        L["st_wxbcT"] = A.at(C0 + 4096, 128, NC * 192, F32)
        L["st_wgT"] = A.at(C0 + 5632, 128, NC * 64, F32)
        L["st_wlogAT"] = A.at(C0 + 8192, 128, NC * 16, F32)
        L["xtile"] = A.at(C0, 128, 2 * D, F32)
        L["xc_t"] = A.at(C0 + 2048, 128, D, F32)
        L["sq_t"] = A.at(C0 + 3072, 128, D, F32)
        L["xn_t"] = A.at(C0 + 4096, 128, D, F32)
        L["wtile"] = A.at(C0 + 8192, 128, 2 * 1024, F32)
        L["sel16"] = A.at(C0 + 6144, 16, 2048, F32)
        L["ao_t"] = A.at(C0 + 8192, 128, 2 * D, F32)
        L["hstage"] = [A.at(C0 + t * 2048, 128, 2 * D, F32) for t in range(4)]
        L["sgt"] = A.at(C0 + 8192, 128, 2 * 512, F32)
        L["xat"] = A.at(C0 + 9216, 128, 2 * 512, F32)
        L["aat"] = A.at(C0 + 10240, 128, 2 * 512, F32)
        L["gacc"] = sb("gacc", [128, 128])

        bank = [es.enter_context(nc.psum_tensor(f"bank{i}", [128, 512], F32))
                for i in range(8)]
        L["bank"] = bank

        block = es.enter_context(nc.Block())
        emit_all(P, ap, L, dbg_phase)

        @block.sync
        def _(e):
            for f in P.q["sync"]:
                f(e)

        @block.scalar
        def _(e):
            for f in P.q["scalar"]:
                f(e)

        @block.vector
        def _(e):
            for f in P.q["vector"]:
                f(e)

        @block.tensor
        def _(e):
            for f in P.q["tensor"]:
                f(e)

        @block.gpsimd
        def _(e):
            for f in P.q["gpsimd"]:
                f(e)
    return nc


SKIP_QUANT = False


def emit_all(P, ap, L, dbg_phase):
    semh = L["semh"]
    bank = L["bank"]

    def c_load(e):
        pairs = [("wxbcT", L["st_wxbcT"]), ("wgT", L["st_wgT"]), ("wlogAT", L["st_wlogAT"]),
                 ("bBs", L["c_bBs"]), ("bCs", L["c_bCs"]), ("bXs", L["c_bXs"]),
                 ("bgrep", L["c_bgrep"]), ("borep", L["c_borep"]),
                 ("ident", L["c_ident"]), ("identb", L["c_identb"]), ("tri", L["c_tri"]),
                 ("ones128", L["c_ones"]), ("blogA", L["c_blogA"]),
                 ("s_in", L["c_sx"]), ("s_in_down", L["c_sxd"]), ("sel16", L["sel16"]),
                 ("z_16x128", L["zz16"]), ("z_64f", L["S_sb"]), ("z_64b", L["Sb_sb"]),
                 ("z_stmp", L["stmp"]), ("epsrep", L["c_eps"])]
        for nm, dst in pairs:
            e.dma_start(out=dst[:], in_=ap[nm][:, :]).then_inc(semh["sw"], 16)
        e.dma_start(out=L["st_woT"][:], in_=ap["woT"][:, 0:4096]).then_inc(semh["sw"], 16)
    P.on("sync", c_load)
    n_const = P.inc("sw", 16 * 22)

    def c_cast(e, n=n_const):
        e.wait_ge(semh["sw"], n)
        e.tensor_copy(out=L["c_wxbcTb"][:], in_=L["st_wxbcT"][:])
        e.tensor_copy(out=L["c_wgTb"][:], in_=L["st_wgT"][:])
        e.tensor_copy(out=L["c_wlogATb"][:], in_=L["st_wlogAT"][:])
        e.tensor_copy(out=L["c_woTb"][:, 0:4096], in_=L["st_woT"][:])
        e.tensor_scalar(out=L["stmp"][0:1, 0:1], in0=L["c_sx"][0:1, 0:1],
                        scalar1=1.0 / 6.0, scalar2=None, op0=Op.mult)
        e.tensor_scalar(out=L["stmp"][0:1, 1:2], in0=L["c_sx"][0:1, 0:1],
                        scalar1=2.0, scalar2=None, op0=Op.mult).then_inc(semh["lv"], 1)
    P.on("vector", c_cast)
    n_cast0 = P.inc("lv", 1)

    def c_wo1(e, n=n_cast0):
        e.wait_ge(semh["lv"], n)
        e.dma_start(out=L["st_woT"][:], in_=ap["woT"][:, 4096:8192]).then_inc(semh["sw"], 16)
    P.on("sync", c_wo1)
    n_wo1 = P.inc("sw", 16)

    def c_cast1(e, n=n_wo1):
        e.wait_ge(semh["sw"], n)
        e.tensor_copy(out=L["c_woTb"][:, 4096:8192],
                      in_=L["st_woT"][:]).then_inc(semh["lv"], 1)
    P.on("vector", c_cast1)
    n_cast = P.inc("lv", 1)

    def c_bcast_mm(e, n=n_cast):
        e.wait_ge(semh["lv"], n)
        e.matmul(out=bank[7][:, 0:16], lhsT=L["c_ones"][0:1, 0:128],
                 rhs=L["stmp"][0:1, 0:16], start=True, stop=True).then_inc(semh["tb"], 1)
    P.on("tensor", c_bcast_mm)
    n_bcmm = P.inc("tb", 1)

    def c_bcast_cp(e, n=n_bcmm):
        e.wait_ge(semh["tb"], n)
        e.tensor_copy(out=L["PP"][:], in_=bank[7][:, 0:16]).then_inc(semh["lv"], 1)
    P.on("vector", c_bcast_cp)
    n_bc_x = P.inc("lv", 1)

    xtile = L["xtile"]; xc_t = L["xc_t"]; sq_t = L["sq_t"]; xn_t = L["xn_t"]
    xnb_t = L["xnb_t"]; vqxn_t = L["vqxn_t"]; ln_s = L["ln_s"]
    lv_red = []

    for i in range(NT):
        xs = xtile[:, (i % 2) * D:(i % 2 + 1) * D]
        xnb = xnb_t[:, (i % 2) * D:(i % 2 + 1) * D]
        vqx = vqxn_t[:, (i % 2) * D:(i % 2 + 1) * D]

        def s_load(e, i=i, xs=xs, ncast=n_cast):
            if i < 2:
                e.wait_ge(semh["lv"], ncast)
            else:
                e.wait_ge(semh["ls"], 3 * (i - 2) + 1)
            e.dma_start(out=xs, in_=ap["x"][i * 128:(i + 1) * 128, :]).then_inc(semh["sx"], 16)
        P.on("sync", s_load)
        n_x = P.inc("sx", 16)

        nrb_ln = P.inc("rb", 1)
        def v_ln1(e, i=i, xs=xs, n=n_x, nrb=nrb_ln):
            e.wait_ge(semh["sx"], n)
            e.tensor_reduce(out=ln_s[:, 0:1], in_=xs, axis=AX, op=Op.add).then_inc(semh["rb"], 1)
            e.wait_ge(semh["rb"], nrb)
            e.tensor_scalar(out=ln_s[:, 1:2], in0=ln_s[:, 0:1], scalar1=-1.0 / D,
                            scalar2=None, op0=Op.mult).then_inc(semh["lv"], 1)
        P.on("vector", v_ln1)
        a_v = P.inc("lv", 1)

        def s_ln1(e, xs=xs, n=a_v):
            e.wait_ge(semh["lv"], n)
            e.activation(out=xc_t[:], in_=xs, func=AF.Identity, bias=ln_s[:, 1:2], scale=1.0)
            e.activation(out=sq_t[:], in_=xc_t[:], func=AF.Square).then_inc(semh["ls"], 1)
        P.on("scalar", s_ln1)
        a_s = P.inc("ls", 1)

        def v_ln2(e, n=a_s):
            e.wait_ge(semh["ls"], n)
            e.tensor_reduce(out=ln_s[:, 2:3], in_=sq_t[:], axis=AX, op=Op.add).then_inc(semh["lv"], 1)
        P.on("vector", v_ln2)
        b_v = P.inc("lv", 1)

        def s_ln2(e, n=b_v):
            e.wait_ge(semh["lv"], n)
            e.activation(out=ln_s[:, 3:4], in_=ln_s[:, 2:3], func=AF.Sqrt,
                         bias=L["c_eps"][:], scale=1.0 / D).then_inc(semh["ls"], 1)
        P.on("scalar", s_ln2)
        b_s = P.inc("ls", 1)

        def v_ln3(e, n=b_s):
            e.wait_ge(semh["ls"], n)
            e.reciprocal(out=ln_s[:, 0:1], in_=ln_s[:, 3:4]).then_inc(semh["lv"], 1)
        P.on("vector", v_ln3)
        c_v = P.inc("lv", 1)

        def s_ln3(e, n=c_v):
            e.wait_ge(semh["lv"], n)
            e.activation(out=xn_t[:], in_=xc_t[:], func=AF.Identity,
                         bias=0.0, scale=ln_s[:, 0:1]).then_inc(semh["ls"], 1)
        P.on("scalar", s_ln3)
        c_s = P.inc("ls", 1)

        def v_xnb(e, i=i, xnb=xnb, n=c_s):
            e.wait_ge(semh["ls"], n)
            if i >= 2:
                e.wait_ge(semh["scr1"], 16 * (2 * (i - 2) + 1))
            e.tensor_copy(out=xnb, in_=xn_t[:]).then_inc(semh["lv"], 1)
        P.on("vector", v_xnb)
        d_v = P.inc("lv", 1)

        def v_qpre(e, i=i, n=d_v):
            if i >= 2:
                e.wait_ge(semh["scr1"], 16 * (2 * (i - 2) + 2))
        P.on("vector", v_qpre)
        if SKIP_QUANT:
            def v_qfake(e, vqx=vqx):
                e.tensor_copy(out=vqx, in_=xn_t[:]).then_inc(semh["qv"], 2)
            P.on("vector", v_qfake)
            n_vq = P.inc("qv", 2)
        else:
            n_vq = emit_quant_v(P, semh, xn_t[:], D, L["qt"],
                                L["PP"][:, 0:1], L["PP"][:, 1:2], vqx)

        def s_dma(e, i=i, xnb=xnb, vqx=vqx, nd=d_v, nq=n_vq):
            e.wait_ge(semh["lv"], nd)
            e.dma_start(out=ap["xn_scr"][i * 128:(i + 1) * 128, :], in_=xnb).then_inc(semh["scr1"], 16)
            e.wait_ge(semh["qv"], nq)
            e.dma_start(out=ap["vqxn_scr"][i * 128:(i + 1) * 128, :], in_=vqx).then_inc(semh["scr1"], 16)
        P.on("sync", s_dma)
        P.inc("scr1", 32)

        for j in range(6 * i, 6 * i + 6):
            wn1, qn1, ti1, q1, _c6, _c2, _wi = W_CHUNKS[j]
            sl1 = L["wtile"][:, (j % 2) * 1024:(j % 2) * 1024 + 1024]

            def s_w1(e, wn1=wn1, ti1=ti1, q1=q1, sl1=sl1, j=j, ncast=n_cast,
                     lvw=(lv_red[j - 2] if j >= 2 else None)):
                if j < 2:
                    e.wait_ge(semh["lv"], ncast)
                else:
                    e.wait_ge(semh["lv"], lvw)
                e.dma_start(out=sl1, in_=ap[wn1][ti1 * 128:(ti1 + 1) * 128,
                                                 q1 * 1024:(q1 + 1) * 1024]).then_inc(semh["sw"], 16)
            P.on("sync", s_w1)
            n_l1 = P.inc("sw", 16)

            def v_g1(e, j=j, sl1=sl1, n=n_l1):
                e.wait_ge(semh["sw"], n)
                e.tensor_reduce(out=L["gacc"][:, j:j + 1], in_=sl1, axis=AX, op=Op.max,
                                apply_absolute_value=True).then_inc(semh["lv"], 1)
            P.on("vector", v_g1)
            lv_red.append(P.inc("lv", 1))

    def s_tr(e, n=P.val("scr1")):
        e.wait_ge(semh["scr1"], n)
        for c in range(NC):
            e.dma_start(out=L["xnT"][:, c * T:(c + 1) * T],
                        in_=ap["xn_scr"][:, c * 128:(c + 1) * 128],
                        transpose=True).then_inc(semh["str1"], 16)
    P.on("sync", s_tr)
    n_tr1 = P.inc("str1", 16 * 8)

    if dbg_phase == "ln":
        def g_fin(e, n=P.val("scr1"), n2=n_tr1):
            e.wait_ge(semh["scr1"], n)
            e.wait_ge(semh["str1"], n2)
        P.on("gpsimd", g_fin)
        return

    L["lv_pass1_done"] = lv_red[-1]
    emit_wscales(P, ap, L)
    emit_mamba(P, ap, L, dbg_phase)
    if dbg_phase == "mamba":
        return
    emit_ffn(P, ap, L, dbg_phase)


def emit_wscales(P, ap, L):
    semh = L["semh"]; bank = L["bank"]; stmp = L["stmp"]; PP = L["PP"]

    for wi in range(3):
        def v_gf(e, wi=wi, nrb=P.inc("rb", 1)):
            e.nop().then_inc(semh["rb"], 1)
            e.wait_ge(semh["rb"], nrb)
            e.tensor_reduce(out=L["wred"][:], in_=L["gacc"][:, wi * 32:(wi + 1) * 32],
                            axis=AX, op=Op.max).then_inc(semh["lv"], 1)
        P.on("vector", v_gf)
        n_lv = P.inc("lv", 1)

        def t_gmax(e, n=n_lv):
            e.wait_ge(semh["lv"], n)
            e.transpose(out=bank[7][0:1, 0:128], in_=L["wred"][:, 0:1],
                        identity=L["c_ident"][:, :]).then_inc(semh["tb"], 1)
        P.on("tensor", t_gmax)
        n_t = P.inc("tb", 1)

        def v_gmax2(e, wi=wi, n=n_t, nrb=P.inc("rb", 1)):
            e.wait_ge(semh["tb"], n)
            e.tensor_reduce(out=stmp[0:1, 13 + wi:14 + wi], in_=bank[7][0:1, 0:128],
                            axis=AX, op=Op.max).then_inc(semh["rb"], 1)
            e.wait_ge(semh["rb"], nrb)
        P.on("vector", v_gmax2)

    def v_scal(e, nrb=P.inc("rb", 6)):
        for wi in range(3):
            e.reciprocal(out=stmp[0:1, 16 + wi:17 + wi],
                         in_=stmp[0:1, 13 + wi:14 + wi]).then_inc(semh["rb"], 1)
        e.wait_ge(semh["rb"], nrb - 3)
        for wi in range(3):
            e.tensor_scalar(out=stmp[0:1, 19 + wi:20 + wi], in0=stmp[0:1, 16 + wi:17 + wi],
                            scalar1=2688.0, scalar2=None, op0=Op.mult)
        li = e.nop().then_inc(semh["rb"], 1)
        e.wait_ge(semh["rb"], nrb - 2)
        for wi, (c6, c2) in enumerate([(7, 8), (9, 10), (11, 12)]):
            e.tensor_scalar(out=stmp[0:1, c6:c6 + 1], in0=stmp[0:1, 19 + wi:20 + wi],
                            scalar1=1.0 / 6.0, scalar2=None, op0=Op.mult)
            e.tensor_scalar(out=stmp[0:1, c2:c2 + 1], in0=stmp[0:1, 19 + wi:20 + wi],
                            scalar1=2.0, scalar2=None, op0=Op.mult)
        e.tensor_tensor(out=stmp[0:1, 22:23], in0=stmp[0:1, 19:20],
                        in1=L["c_sx"][0:1, 0:1], op=Op.mult)
        e.tensor_tensor(out=stmp[0:1, 23:24], in0=stmp[0:1, 20:21],
                        in1=L["c_sx"][0:1, 0:1], op=Op.mult)
        e.tensor_tensor(out=stmp[0:1, 24:25], in0=stmp[0:1, 21:22],
                        in1=L["c_sxd"][0:1, 0:1], op=Op.mult)
        e.nop().then_inc(semh["rb"], 1)
        e.wait_ge(semh["rb"], nrb - 1)
        e.reciprocal(out=stmp[0:1, 4:5], in_=stmp[0:1, 22:23])
        e.reciprocal(out=stmp[0:1, 5:6], in_=stmp[0:1, 23:24])
        e.reciprocal(out=stmp[0:1, 6:7], in_=stmp[0:1, 24:25]).then_inc(semh["rb"], 1)
        e.wait_ge(semh["rb"], nrb)
        e.tensor_tensor(out=stmp[0:1, 25:26], in0=stmp[0:1, 5:6],
                        in1=L["c_sxd"][0:1, 0:1], op=Op.mult)
        e.nop().then_inc(semh["rb"], 1)
        e.wait_ge(semh["rb"], nrb + 1)
        e.tensor_scalar(out=stmp[0:1, 2:3], in0=stmp[0:1, 25:26], scalar1=1.0 / 6.0,
                        scalar2=None, op0=Op.mult)
        e.tensor_scalar(out=stmp[0:1, 3:4], in0=stmp[0:1, 25:26], scalar1=2.0,
                        scalar2=None, op0=Op.mult)
        e.nop().then_inc(semh["lv"], 1)
    P.on("vector", v_scal)
    P.inc("rb", 1)
    n_lv2 = P.inc("lv", 1)

    def t_pp2(e, n=n_lv2):
        e.wait_ge(semh["lv"], n)
        e.matmul(out=bank[7][:, 0:16], lhsT=L["c_ones"][0:1, 0:128],
                 rhs=stmp[0:1, 0:16], start=True, stop=True).then_inc(semh["tb"], 1)
    P.on("tensor", t_pp2)
    n_tpp = P.inc("tb", 1)

    def v_pp2(e, n=n_tpp):
        e.wait_ge(semh["tb"], n)
        e.tensor_copy(out=PP[:], in_=bank[7][:, 0:16]).then_inc(semh["lv"], 1)
    P.on("vector", v_pp2)
    L["n_pp2"] = P.inc("lv", 1)


def emit_mamba(P, ap, L, dbg_phase):
    semh = L["semh"]; bank = L["bank"]
    xnT = L["xnT"]; dec = L["dec_sb"]; logA = L["logA_sb"]; ecT = L["ecT_sb"]
    Bt2, Ct2, Xt2 = L["Bt2"], L["Ct2"], L["Xt2"]
    Dt2, Ptmp, Pm2 = L["Dt2"], L["Ptmp"], L["Pm2"]
    Xk2, Bsc2, Csc2, gate_s2 = L["Xk2"], L["Bsc2"], L["Csc2"], L["gate_s2"]
    S_sb, Sb_sb, Stmp = L["S_sb"], L["Sb_sb"], L["Stmp"]
    decColT2, negdecT2, ebpT2 = L["decColT2"], L["negdecT2"], L["ebpT2"]
    lamrow2, lamT2 = L["lamrow2"], L["lamT2"]
    ygr = L["ygring"]
    ygbase = P.val("scr1")

    n_xnT_tr = 16 * 8

    for tc in range(4):
        def t_loga(e, tc=tc, w=P.val("ls")):
            if tc == 0:
                e.wait_ge(semh["str1"], n_xnT_tr)
            else:
                e.wait_ge(semh["ls"], w)
            for c in range(NC):
                mm = e.matmul(out=bank[0][0:16, 0:512],
                              lhsT=L["c_wlogATb"][:, c * 16:(c + 1) * 16],
                              rhs=xnT[:, c * T + tc * 512: c * T + (tc + 1) * 512],
                              start=(c == 0), stop=(c == NC - 1))
                if c == NC - 1:
                    mm.then_inc(semh["tb"], 1)
        P.on("tensor", t_loga)
        n_t = P.inc("tb", 1)

        def s_loga(e, tc=tc, n=n_t):
            e.wait_ge(semh["tb"], n)
            e.activation(out=logA[0:16, tc * 512:(tc + 1) * 512], in_=bank[0][0:16, 0:512],
                         func=AF.Identity, bias=L["c_blogA"][0:16, 0:1],
                         scale=1.0).then_inc(semh["ls"], 1)
        P.on("scalar", s_loga)
        P.inc("ls", 1)

    n_loga = P.val("ls")

    def v_dec(e, n=n_loga):
        e.wait_ge(semh["ls"], n)
        for k in range(16):
            li = e.tensor_tensor_scan(out=dec[0:16, k * 128:(k + 1) * 128],
                                      data0=logA[0:16, k * 128:(k + 1) * 128],
                                      data1=L["zz16"][0:16, 0:128], initial=0.0,
                                      op0=Op.add, op1=Op.add)
        li.then_inc(semh["lv"], 1)
    P.on("vector", v_dec)
    n_dec = P.inc("lv", 1)

    def s_ec(e, n=n_dec):
        e.wait_ge(semh["lv"], n)
        e.activation(out=ecT[0:16, :], in_=dec[0:16, :], func=AF.Exp).then_inc(semh["ls"], 1)
    P.on("scalar", s_ec)
    n_ec = P.inc("ls", 1)

    tb0 = P.val("tb"); vb0 = P.val("vb"); sb0 = P.val("sb_")
    NB = 259

    wx = L["c_wxbcTb"]; wg = L["c_wgTb"]

    w2_sched = [("load", 0), ("load", 1)]
    for _j in range(96):
        w2_sched.append(("seg1", _j))
        if _j + 2 < 96:
            w2_sched.append(("load", _j + 2))
        w2_sched.append(("seg2", _j))
        w2_sched.append(("seg3", _j))
        w2_sched.append(("seg4", _j))
        w2_sched.append(("tpose", _j))
        w2_sched.append(("store", _j))
    w2_state = {"ptr": 0, "sw": {}, "qg": {}, "gy": {}, "qv": {}, "wt": {},
                "segs": {}}
    tr_after = {}

    def w2_emit_one():
        kind, j = w2_sched[w2_state["ptr"]]
        w2_state["ptr"] += 1
        wn, qn, ti, q, c6, c2, wi = W_CHUNKS[j]
        sl = L["wtile"][:, (j % 2) * 1024:(j % 2) * 1024 + 1024]
        if kind == "load":
            def s_w2(e, wn=wn, ti=ti, q=q, sl=sl, j=j, lvp=L["lv_pass1_done"],
                     qgw=(w2_state["qg"].get(j - 2))):
                if j < 2:
                    e.wait_ge(semh["lv"], lvp)
                else:
                    e.wait_ge(semh["qg"], qgw)
                e.dma_start(out=sl, in_=ap[wn][ti * 128:(ti + 1) * 128,
                                               q * 1024:(q + 1) * 1024]).then_inc(semh["sw"], 16)
            P.on("sync", s_w2)
            w2_state["sw"][j] = P.inc("sw", 16)
        elif kind == "seg1":
            segs = emit_quant_vecseg(P, semh, sl, 1024, L["qt"],
                                     L["PP"][:, c6:c6 + 1], L["PP"][:, c2:c2 + 1],
                                     L["wvqb"][:, 0:1024],
                                     sw_wait=("sw", w2_state["sw"][j]))
            w2_state["segs"][j] = segs
            P.on("vector", segs[0])
            w2_state["qg"][j] = P.inc("qg", 1)
        elif kind == "seg2":
            P.on("vector", w2_state["segs"][j][1])
        elif kind == "seg3":
            P.on("vector", w2_state["segs"][j][2])
        elif kind == "seg4":
            nwt = w2_state["wt"].get(j - 1)
            if nwt is not None:
                def v_w4pre(e, nwt=nwt):
                    e.wait_ge(semh["wt"], nwt)
                P.on("vector", v_w4pre)
            P.on("vector", w2_state["segs"][j][3])
            w2_state["qv"][j] = P.inc("qv", 1)
            del w2_state["segs"][j]
        elif kind == "tpose":
            def s_tp2(e, n=w2_state["qv"][j], ngy=w2_state["gy"].get(j - 1)):
                e.wait_ge(semh["qv"], n)
                if ngy is not None:
                    e.wait_ge(semh["gy"], ngy)
                for d in range(8):
                    e.dma_start(out=L["wvqT"][:, d * 128:(d + 1) * 128],
                                in_=L["wvqb"][:, d * 128:(d + 1) * 128],
                                transpose=True).then_inc(semh["wt"], 16)
            P.on("sync", s_tp2)
            w2_state["wt"][j] = P.inc("wt", 16 * 8)
        elif kind == "store":
            if qn in ("wq_g", "wq_u"):
                out_ap = ap[qn][:, ti * 128:(ti + 1) * 128]
            else:
                out_ap = ap[qn][q * 1024:(q + 1) * 1024, ti * 128:(ti + 1) * 128]

            def s_st2(e, out_ap=out_ap, n=w2_state["wt"][j]):
                e.wait_ge(semh["wt"], n)
                e.dma_start(out=out_ap.rearrange("(d p) c -> p d c", p=128),
                            in_=L["wvqT"][:, 0:1024].rearrange("p (d c) -> p d c", c=128)
                            ).then_inc(semh["gy"], 16)
            P.on("sync", s_st2)
            w2_state["gy"][j] = P.inc("gy", 16)

    for B in range(NB):
        def t_beat(e, B=B, tb0=tb0, vb0=vb0, sb0=sb0, ndec=n_dec, nec=n_ec,
                   npp=L["n_pp2"], ntr3=tr_after.get(B - 3)):
            if B == 0:
                e.wait_ge(semh["lv"], ndec)
                e.wait_ge(semh["ls"], nec)
                e.wait_ge(semh["lv"], npp)
            if B >= 1:
                e.wait_ge(semh["vb"], vb0 + B)
                e.wait_ge(semh["sb_"], sb0 + B)
            pb = B % 2
            li = None
            if B % 16 == 0 and B < 256:
                k = B // 16
                li = e.transpose(out=bank[2 + pb][:, 384:400],
                            in_=dec[0:16, k * 128:(k + 1) * 128],
                            identity=L["c_ident"][0:16, 0:16])
            if B >= 2 and (B - 2) % 16 == 0 and (B - 2) // 16 < 16:
                k = (B - 2) // 16
                li = e.matmul(out=bank[2 + pb][:, 400:416], lhsT=L["c_ones"][0:16, 0:128],
                         rhs=L["diag16"][0:16, 0:16], start=True, stop=True)
            j3 = B - 3
            if 0 <= j3 < 256:
                k, h = divmod(j3, 16); pj = j3 % 2; p3 = j3 % 3
                e.wait_ge(semh["str1"], ntr3)
                li = e.matmul(out=bank[6 + pb][:, 0:64], lhsT=Pm2[:, pj * 128:(pj + 1) * 128],
                         rhs=Xk2[:, p3 * 64:(p3 + 1) * 64], start=True, stop=False)
                li = e.matmul(out=bank[6 + pb][:, 0:64], lhsT=Csc2[0:64, pj * 128:(pj + 1) * 128],
                         rhs=Sb_sb[0:64, h * 64:(h + 1) * 64], start=False, stop=True)
                li = e.matmul(out=bank[6 + pb][0:64, 64:128], lhsT=Bsc2[:, pj * 64:(pj + 1) * 64],
                         rhs=Xk2[:, p3 * 64:(p3 + 1) * 64], start=True, stop=True)
            j2 = B - 2
            if 0 <= j2 < 256:
                k, h = divmod(j2, 16); pj = j2 % 2
                li = e.matmul(out=bank[2 + pb][:, 0:128],
                         lhsT=L["sel16"][0:16, h * 128:(h + 1) * 128],
                         rhs=dec[0:16, k * 128:(k + 1) * 128], start=True, stop=True)
                li = e.matmul(out=bank[2 + pb][:, 128:256], lhsT=Bt2[0:64, pj * 128:(pj + 1) * 128],
                         rhs=Ct2[0:64, pj * 128:(pj + 1) * 128], start=True, stop=True)
                li = e.matmul(out=bank[2 + pb][0:64, 256:384],
                         lhsT=L["sel16"][0:16, h * 128: h * 128 + 64],
                         rhs=ecT[0:16, k * 128:(k + 1) * 128], start=True, stop=True)
            j1 = B - 1
            if 0 <= j1 < 256:
                k, h = divmod(j1, 16); pj = j1 % 2
                cc = h // 2; po = 64 * (h % 2)
                xn_sl = xnT[po:po + 64, cc * T + k * 128: cc * T + (k + 1) * 128]
                li = e.matmul(out=bank[pb][0:64, 0:128],
                         lhsT=wx[po:po + 64, cc * 192 + 64: cc * 192 + 128],
                         rhs=xn_sl, start=True, stop=True)
                li = e.matmul(out=bank[pb][0:64, 128:256],
                         lhsT=wx[po:po + 64, cc * 192 + 128: cc * 192 + 192],
                         rhs=xn_sl, start=True, stop=True)
                li = e.matmul(out=bank[pb][0:64, 256:384],
                         lhsT=wx[po:po + 64, cc * 192: cc * 192 + 64],
                         rhs=xn_sl, start=True, stop=True)
                li = e.matmul(out=bank[pb][:, 384:448], lhsT=xn_sl,
                         rhs=wg[po:po + 64, cc * 64:(cc + 1) * 64], start=True, stop=True)
            (li if li is not None else e.nop()).then_inc(semh["tb"], 1)
        P.on("tensor", t_beat)
        P.inc("tb", 1)

        def s_beat(e, B=B, tb0=tb0, vb0=vb0):
            e.wait_ge(semh["tb"], tb0 + B + 1)
            if B >= 1:
                e.wait_ge(semh["vb"], vb0 + B)
            pb = B % 2
            li = None
            if B >= 1 and (B - 1) % 16 == 0 and (B - 1) // 16 < 16:
                k = (B - 1) // 16; pk = k % 2
                li = e.activation(out=ebpT2[:, pk * 16:(pk + 1) * 16],
                             in_=negdecT2[:, pk * 16:(pk + 1) * 16], func=AF.Exp)
            if B >= 2 and (B - 2) % 16 == 0 and (B - 2) // 16 < 16:
                k = (B - 2) // 16; pk = k % 2
                li = e.activation(out=lamT2[:, pk * 16:(pk + 1) * 16],
                             in_=bank[2 + pb][:, 400:416], func=AF.Exp)
            j1 = B - 1
            if 0 <= j1 < 256:
                k, h = divmod(j1, 16); pj = j1 % 2; p3 = j1 % 3
                li = e.activation(out=Bt2[0:64, pj * 128:(pj + 1) * 128], in_=bank[pb][0:64, 0:128],
                             func=AF.Silu, bias=L["c_bBs"][0:64, h:h + 1], scale=1.0)
                li = e.activation(out=Ct2[0:64, pj * 128:(pj + 1) * 128], in_=bank[pb][0:64, 128:256],
                             func=AF.Silu, bias=L["c_bCs"][0:64, h:h + 1], scale=1.0)
                li = e.activation(out=Xt2[0:64, pj * 128:(pj + 1) * 128], in_=bank[pb][0:64, 256:384],
                             func=AF.Silu, bias=L["c_bXs"][0:64, h:h + 1], scale=1.0)
            j2 = B - 2
            if 0 <= j2 < 256:
                k, h = divmod(j2, 16); pj = j2 % 2; pk = k % 2
                li = e.activation(out=Dt2[:, pj * 128:(pj + 1) * 128], in_=bank[2 + pb][:, 0:128],
                             func=AF.Exp, bias=negdecT2[:, pk * 16 + h: pk * 16 + h + 1],
                             scale=1.0)
            (li if li is not None else e.nop()).then_inc(semh["sb_"], 1)
        P.on("scalar", s_beat)
        P.inc("sb_", 1)

        if 0 <= B - 1 < 256:
            j1s = B - 1

            def s_btr(e, j1s=j1s, B=B, sb0=sb0):
                pj = j1s % 2; p3 = j1s % 3
                e.wait_ge(semh["sb_"], sb0 + B + 1)
                e.dma_start(out=L["btT"][:, pj * 64:(pj + 1) * 64],
                            in_=Bt2[0:64, pj * 128:(pj + 1) * 128],
                            transpose=True).then_inc(semh["str1"], 16)
                e.dma_start(out=Xk2[:, p3 * 64:(p3 + 1) * 64],
                            in_=Xt2[0:64, pj * 128:(pj + 1) * 128],
                            transpose=True).then_inc(semh["str1"], 16)
            P.on("sync", s_btr)
            tr_after[B - 1] = P.inc("str1", 32)

        nrb_ch = P.inc("rb", 1) if (B % 16 == 0 and B < 256) else None
        def v_beat(e, B=B, tb0=tb0, sb0=sb0, nrb=nrb_ch,
                   ntr=tr_after.get(B - 2)):
            e.wait_ge(semh["tb"], tb0 + B + 1)
            e.wait_ge(semh["sb_"], sb0 + B + 1)
            pb = B % 2
            li = None
            j3 = B - 3
            if 0 <= j3 < 256:
                k, h = divmod(j3, 16); pj = j3 % 2; pk = k % 2
                if h == 0 and k >= 3:
                    e.wait_ge(semh["scr1"], ygbase + 16 * (k - 2))
                li = e.tensor_tensor(out=ygr[:, (k % 3) * D + h * 64: (k % 3) * D + (h + 1) * 64],
                                in0=gate_s2[:, pj * 64:(pj + 1) * 64],
                                in1=bank[6 + pb][:, 0:64], op=Op.mult)
                li = e.tensor_tensor(out=Stmp[0:64, 0:64], in0=S_sb[0:64, h * 64:(h + 1) * 64],
                                in1=bank[6 + pb][0:64, 64:128], op=Op.add)
            if B % 16 == 0 and B < 256:
                k = B // 16; pk = k % 2
                li = e.tensor_copy(out=decColT2[:, pk * 16:(pk + 1) * 16],
                                   in_=bank[2 + pb][:, 384:400])
                li.then_inc(semh["rb"], 1)
                li = e.tensor_scalar(out=L["diag16"][0:16, 0:16], in0=L["c_ident"][0:16, 0:16],
                                scalar1=dec[0:16, k * 128 + 127: k * 128 + 128],
                                scalar2=None, op0=Op.mult)
                e.wait_ge(semh["rb"], nrb)
                li = e.tensor_scalar(out=negdecT2[:, pk * 16:(pk + 1) * 16],
                                in0=decColT2[:, pk * 16:(pk + 1) * 16],
                                scalar1=-1.0, scalar2=None, op0=Op.mult)
            j2 = B - 2
            if 0 <= j2 < 256:
                k, h = divmod(j2, 16); pj = j2 % 2; pk = k % 2
                li = e.tensor_tensor(out=Ptmp[:], in0=Dt2[:, pj * 128:(pj + 1) * 128],
                                in1=bank[2 + pb][:, 128:256], op=Op.mult)
                e.wait_ge(semh["str1"], ntr)
                li = e.tensor_scalar(out=Bsc2[:, pj * 64:(pj + 1) * 64],
                                in0=L["btT"][:, pj * 64:(pj + 1) * 64],
                                scalar1=ebpT2[:, pk * 16 + h: pk * 16 + h + 1],
                                scalar2=None, op0=Op.mult)
                li = e.tensor_tensor(out=Csc2[0:64, pj * 128:(pj + 1) * 128],
                                in0=Ct2[0:64, pj * 128:(pj + 1) * 128],
                                in1=bank[2 + pb][0:64, 256:384], op=Op.mult)
                li = e.tensor_tensor(out=Pm2[:, pj * 128:(pj + 1) * 128], in0=Ptmp[:],
                                in1=L["c_tri"][:], op=Op.mult)
            j1 = B - 1
            if 0 <= j1 < 256:
                k, h = divmod(j1, 16); pj = j1 % 2
                li = e.tensor_tensor(out=gate_s2[:, pj * 64:(pj + 1) * 64],
                                in0=bank[pb][:, 384:448],
                                in1=L["c_bgrep"][:, h * 64:(h + 1) * 64], op=Op.add)
            if 0 <= j3 < 256:
                k, h = divmod(j3, 16); pk = k % 2
                li = e.tensor_scalar(out=S_sb[0:64, h * 64:(h + 1) * 64], in0=Stmp[0:64, 0:64],
                                scalar1=lamT2[0:64, pk * 16 + h: pk * 16 + h + 1],
                                scalar2=None, op0=Op.mult)
                li = e.tensor_copy(out=Sb_sb[0:64, h * 64:(h + 1) * 64],
                              in_=S_sb[0:64, h * 64:(h + 1) * 64])
            (li if li is not None else e.nop()).then_inc(semh["vb"], 1)
        P.on("vector", v_beat)
        P.inc("vb", 1)

        quota = ((B + 1) * len(w2_sched)) // 250
        while w2_state["ptr"] < min(quota, len(w2_sched)):
            w2_emit_one()
        if B >= 19 and (B - 19) % 16 == 0 and (B - 19) // 16 < 15:
            kst = (B - 19) // 16

            def s_ygk(e, kst=kst, vb0=vb0):
                e.wait_ge(semh["vb"], vb0 + 16 * kst + 19)
                e.dma_start(out=ap["yg_scr"][kst * 128:(kst + 1) * 128, :],
                            in_=ygr[:, (kst % 3) * D:(kst % 3 + 1) * D]).then_inc(semh["scr1"], 16)
            P.on("sync", s_ygk)
            P.inc("scr1", 16)

    while w2_state["ptr"] < len(w2_sched):
        w2_emit_one()

    def s_ygk15(e, vb0=vb0):
        e.wait_ge(semh["vb"], vb0 + 16 * 15 + 19)
        e.dma_start(out=ap["yg_scr"][15 * 128:16 * 128, :],
                    in_=ygr[:, (15 % 3) * D:(15 % 3 + 1) * D]).then_inc(semh["scr1"], 16)
    P.on("sync", s_ygk15)
    P.inc("scr1", 16)

    n_scan_vb = P.val("vb")

    if dbg_phase == "mamba" and "dbg_dec" in ap:
        def g_dbg(e, n=n_scan_vb):
            e.wait_ge(semh["vb"], n)
            e.dma_start(out=ap["dbg_dec"][0:16, 0:T], in_=dec[0:16, :]).then_inc(semh["gq"], 16)
            e.dma_start(out=ap["dbg_dec"][0:16, T:2 * T], in_=logA[0:16, :]).then_inc(semh["gq"], 16)
            e.dma_start(out=ap["dbg_t"][:, 0:256], in_=Dt2[:]).then_inc(semh["gq"], 16)
            e.dma_start(out=ap["dbg_t"][:, 256:384],
                        in_=Pm2[:].bitcast(F32)).then_inc(semh["gq"], 16)
            e.dma_start(out=ap["dbg_t"][0:64, 384:512],
                        in_=Bt2[:].bitcast(F32)).then_inc(semh["gq"], 16)
            e.dma_start(out=ap["dbg_t"][0:64, 512:640],
                        in_=Ct2[:].bitcast(F32)).then_inc(semh["gq"], 16)
            e.dma_start(out=ap["dbg_t"][0:64, 640:768],
                        in_=Xt2[:].bitcast(F32)).then_inc(semh["gq"], 16)
            e.dma_start(out=ap["dbg_t"][:, 768:832],
                        in_=Xk2[:].bitcast(F32)).then_inc(semh["gq"], 16)
            e.dma_start(out=ap["dbg_t"][:, 832:896],
                        in_=Bsc2[:].bitcast(F32)).then_inc(semh["gq"], 16)
            e.dma_start(out=ap["dbg_t"][0:64, 896:1024],
                        in_=Csc2[:].bitcast(F32)).then_inc(semh["gq"], 16)
            e.dma_start(out=ap["dbg_t"][:, 1024:1152], in_=gate_s2[:]).then_inc(semh["gq"], 16)
            e.dma_start(out=ap["dbg_t"][0:64, 1152:2176],
                        in_=S_sb[:]).then_inc(semh["gq"], 16)
            e.dma_start(out=ap["dbg_t"][:, 2176:2208],
                        in_=L["ebpT2"][:]).then_inc(semh["gq"], 16)
            e.dma_start(out=ap["dbg_t"][:, 2208:2240],
                        in_=L["lamT2"][:]).then_inc(semh["gq"], 16)
            e.dma_start(out=ap["dbg_t"][:, 2240:2272],
                        in_=L["decColT2"][:]).then_inc(semh["gq"], 16)
        P.on("gpsimd", g_dbg)
        P.inc("gq", 16 * 15)

    n_ygw = P.val("scr1")

    def s_ygT(e, n=n_ygw):
        e.wait_ge(semh["scr1"], n)
        for c in range(NC):
            e.dma_start(out=L["ygT"][:, c * T:(c + 1) * T],
                        in_=ap["yg_scr"][:, c * 128:(c + 1) * 128],
                        transpose=True).then_inc(semh["str1"], 16)
    P.on("sync", s_ygT)
    n_ygT = P.inc("str1", 16 * NC)

    sout0 = P.val("sout")
    for i in range(NT):
        def t_op(e, i=i, n=n_ygT, w=P.val("m4")):
            if i == 0:
                e.wait_ge(semh["str1"], n)
            if i >= 1:
                e.wait_ge(semh["m4"], w)
            for half in range(2):
                for c in range(NC):
                    li = e.matmul(out=bank[half][:, 0:512],
                                  lhsT=L["ygT"][:, c * T + i * 128: c * T + (i + 1) * 128],
                                  rhs=L["c_woTb"][:, c * D + half * 512: c * D + half * 512 + 512],
                                  start=(c == 0), stop=(c == NC - 1))
            li.then_inc(semh["tb"], 1)
        P.on("tensor", t_op)
        n_t = P.inc("tb", 1)

        def v_op(e, i=i, n=n_t, s0=sout0):
            e.wait_ge(semh["tb"], n)
            if i >= 2:
                e.wait_ge(semh["sout"], s0 + 16 * (i - 1))
            for half in range(2):
                li = e.scalar_tensor_tensor(out=L["ao_t"][:, (i % 2) * D + half * 512:(i % 2) * D + half * 512 + 512],
                                             in0=bank[half][:, 0:512], scalar=0.0,
                                             in1=L["c_borep"][:, half * 512:half * 512 + 512],
                                             op0=Op.add, op1=Op.add)
            li.then_inc(semh["m4"], 2)
        P.on("vector", v_op)
        n_v = P.inc("m4", 2)

        def s_st(e, i=i, n=n_v):
            e.wait_ge(semh["m4"], n)
            e.dma_start(out=ap["attn_scr"][i * 128:(i + 1) * 128, :],
                        in_=L["ao_t"][:, (i % 2) * D:(i % 2 + 1) * D]).then_inc(semh["sout"], 16)
        P.on("scalar", s_st)
        P.inc("sout", 16)

    if dbg_phase == "mamba":
        def g_fin(e, n=P.val("sout"), nq=P.val("gq")):
            e.wait_ge(semh["sout"], n)
            if nq:
                e.wait_ge(semh["gq"], nq)
        P.on("gpsimd", g_fin)


def emit_ffn(P, ap, L, dbg_phase):
    semh = L["semh"]; bank = L["bank"]
    PP = L["PP"]; qt = L["qt"]
    vqhT = L["vqhT"]
    stmp = L["stmp"]

    hstage = L["hstage"]; sgt = L["sgt"]; gwb = L["gwb"]; uwb = L["uwb"]; dwb = L["dwb"]
    xat, aat = L["xat"], L["aat"]
    vqh_qv = L["vqh_qv"].ap()
    n_pp2 = L["n_pp2"]
    n_wq_done = P.val("gy")

    vqxnT2 = L["vqxnT2"]
    vq_qg = {}
    vq_gy = {}
    for su in range(NSUP):
        vq_tr = {}
        def s_vx(e, su=su, tprev=P.val("tb"), first=(su == 0), nw=n_wq_done,
                 nyg=P.val("scr1")):
            if first:
                e.wait_ge(semh["gy"], nw)
                e.wait_ge(semh["scr1"], nyg)
            if su >= 2:
                e.wait_ge(semh["tb"], tprev)
            for c in range(NC):
                e.dma_start(out=vqxnT2[:, (su % 2) * 4096 + c * 512:(su % 2) * 4096 + (c + 1) * 512],
                            in_=ap["vqxn_scr"][su * 512:(su + 1) * 512, c * 128:(c + 1) * 128],
                            transpose=True).then_inc(semh["str1"], 16)
        P.on("sync", s_vx)
        n_vx = P.inc("str1", 16 * 8)

        for it in range(8):
            for c in range(NC):
                wsl = (it * NC + c) % 4
                def s_gw(e, su=su, it=it, c=c, wsl=wsl, tprev=P.val("tb"),
                         first=(su == 0 and it == 0 and c == 0), nw=n_wq_done):
                    if first:
                        e.wait_ge(semh["gy"], nw)
                    e.wait_ge(semh["tb"], max(tprev - 3, 0))
                    e.dma_start(out=gwb[:, wsl * 512:(wsl + 1) * 512],
                                in_=ap["wq_g"][c * 128:(c + 1) * 128, it * 512:(it + 1) * 512]
                                ).then_inc(semh["sw"], 16)
                    e.dma_start(out=uwb[:, wsl * 512:(wsl + 1) * 512],
                                in_=ap["wq_u"][c * 128:(c + 1) * 128, it * 512:(it + 1) * 512]
                                ).then_inc(semh["sw"], 16)
                P.on("sync", s_gw)
                n_w = P.inc("sw", 32)

                def t_gu(e, su=su, it=it, c=c, wsl=wsl, n=n_w, vprev=P.val("ls"),
                         first=(su == 0 and it == 0 and c == 0), npp=n_pp2,
                         mprev=P.val("m4"), nvx=n_vx):
                    e.wait_ge(semh["sw"], n)
                    if first:
                        e.wait_ge(semh["lv"], npp)
                    if it == 0 and c == 0:
                        e.wait_ge(semh["str1"], nvx)
                    if c == 0 and not (su == 0 and it == 0):
                        e.wait_ge(semh["ls"], vprev)
                    if it == 0 and c == 0:
                        e.wait_ge(semh["m4"], mprev)
                    vb0_ = (su % 2) * 4096
                    for t in range(4):
                        e.matmul(out=bank[t][:, 0:512],
                                 lhsT=vqxnT2[:, vb0_ + c * 512 + t * 128: vb0_ + c * 512 + (t + 1) * 128],
                                 rhs=gwb[:, wsl * 512:(wsl + 1) * 512],
                                 start=(c == 0), stop=(c == NC - 1))
                    for t in range(4):
                        li = e.matmul(out=bank[4 + t][:, 0:512],
                                      lhsT=vqxnT2[:, vb0_ + c * 512 + t * 128: vb0_ + c * 512 + (t + 1) * 128],
                                      rhs=uwb[:, wsl * 512:(wsl + 1) * 512],
                                      start=(c == 0), stop=(c == NC - 1))
                    li.then_inc(semh["tb"], 1)
                P.on("tensor", t_gu)
                P.inc("tb", 1)

            n_tc = P.val("tb")
            for t in range(4):
                gp = (su * 8 + it) // 2
                hoff = (gp % 2) * 1024 + (it % 2) * 512

                def s_silu(e, it=it, t=t, hoff=hoff, n=n_tc, vprev=P.val("f5"),
                           nqg=vq_qg.get((gp - 2, t))):
                    e.wait_ge(semh["tb"], n)
                    if vprev - 1 > 0:
                        e.wait_ge(semh["f5"], vprev - 1)
                    if nqg is not None:
                        e.wait_ge(semh["qg"], nqg)
                    e.activation(out=sgt[:, (t % 2) * 512:(t % 2 + 1) * 512],
                                 in_=bank[t][:, 0:512], func=AF.Silu, bias=0.0,
                                 scale=PP[:, 4:5])
                    e.activation(out=hstage[t][:, hoff:hoff + 512],
                                 in_=bank[4 + t][:, 0:512], func=AF.Identity,
                                 bias=0.0, scale=1.0).then_inc(semh["ls"], 1)
                P.on("scalar", s_silu)
                n_s = P.inc("ls", 1)

                def g_h(e, it=it, t=t, hoff=hoff, n=n_s):
                    e.wait_ge(semh["ls"], n)
                    e.tensor_tensor(out=hstage[t][:, hoff:hoff + 512],
                                    in0=sgt[:, (t % 2) * 512:(t % 2 + 1) * 512],
                                    in1=hstage[t][:, hoff:hoff + 512],
                                    op=Op.mult).then_inc(semh["f5"], 1)
                P.on("gpsimd", g_h)
                P.inc("f5", 1)

            if it % 2 == 1:
                pair = it // 2
                for t in range(4):
                    nch = su * 16 + pair * 4 + t
                    slot = nch % 2

                    def v_q_pre(e, g2=vq_gy.get(su * 16 + pair * 4 + t - 2),
                                ngo=P.val("go"), nf=P.val("f5")):
                        e.wait_ge(semh["f5"], nf)
                        if g2 is not None:
                            e.wait_ge(semh["gy"], g2)
                        if ngo > 0:
                            e.wait_ge(semh["go"], ngo)
                    P.on("vector", v_q_pre)
                    gpq = (su * 4 + pair) % 2
                    segs = emit_quant_vecseg(P, semh,
                                             hstage[t][:, gpq * 1024:(gpq + 1) * 1024],
                                             1024, qt, PP[:, 2:3], PP[:, 3:4],
                                             vqh_qv[:, slot * 1024:(slot + 1) * 1024])
                    P.on("vector", segs[0])
                    vq_qg[(su * 4 + pair, t)] = P.inc("qg", 1)
                    P.on("vector", segs[1])
                    P.on("vector", segs[2])
                    P.on("vector", segs[3])
                    n_q = P.inc("qv", 1)

                    def s_vq(e, t=t, pair=pair, slot=slot, n=n_q):
                        e.wait_ge(semh["qv"], n)
                        e.dma_start(out=ap["vqh_scr"][t * 128:(t + 1) * 128,
                                                      pair * 1024:(pair + 1) * 1024],
                                    in_=vqh_qv[:, slot * 1024:(slot + 1) * 1024]
                                    ).then_inc(semh["gy"], 16)
                    P.on("sync", s_vq)
                    vq_gy[nch] = P.inc("gy", 16)

                def s_vqT(e, pair=pair, n=P.val("gy")):
                    e.wait_ge(semh["gy"], n)
                    for Kc in range(pair * 8, (pair + 1) * 8):
                        e.dma_start(out=vqhT[:, Kc * 512:(Kc + 1) * 512],
                                    in_=ap["vqh_scr"][:, Kc * 128:(Kc + 1) * 128],
                                    transpose=True).then_inc(semh["str1"], 16)
                P.on("sync", s_vqT)
                vq_tr[pair] = P.inc("str1", 16 * 8)

        sout0 = P.val("go")
        for o2 in range(2):
            for Kc in range(32):
                dsl = (o2 * 32 + Kc) % 4
                def s_dw(e, o2=o2, Kc=Kc, dsl=dsl, tprev=P.val("tb")):
                    e.wait_ge(semh["tb"], max(tprev - 3, 0))
                    e.dma_start(out=dwb[:, dsl * 512:(dsl + 1) * 512],
                                in_=ap["wq_d"][Kc * 128:(Kc + 1) * 128, o2 * 512:(o2 + 1) * 512]
                                ).then_inc(semh["sw"], 16)
                P.on("sync", s_dw)
                n_w = P.inc("sw", 16)

                def t_d(e, o2=o2, Kc=Kc, dsl=dsl, su=su, n=n_w,
                        nT=vq_tr[Kc // 8], vprev=P.val("m4"), lsu=P.val("ls")):
                    e.wait_ge(semh["sw"], n)
                    if o2 == 0 and Kc % 8 == 0:
                        e.wait_ge(semh["str1"], nT)
                    if Kc == 0:
                        if o2 == 0:
                            e.wait_ge(semh["ls"], lsu)
                        if not (su == 0 and o2 == 0):
                            e.wait_ge(semh["m4"], vprev)
                    for t in range(4):
                        li = e.matmul(out=bank[t][:, 0:512],
                                      lhsT=vqhT[:, Kc * 512 + t * 128: Kc * 512 + (t + 1) * 128],
                                      rhs=dwb[:, dsl * 512:(dsl + 1) * 512],
                                      start=(Kc == 0), stop=(Kc == 31))
                    li.then_inc(semh["tb"], 1)
                P.on("tensor", t_d)
                P.inc("tb", 1)

            n_td = P.val("tb")
            for t in range(4):
                tok = su * 4 + t
                idx = o2 * 4 + t

                def s_xld(e, tok=tok, t=t, o2=o2, mprev=P.val("m4")):
                    if o2 * 4 + t >= 2:
                        e.wait_ge(semh["m4"], mprev - 1)
                    e.dma_start(out=xat[:, (t % 2) * 512:(t % 2 + 1) * 512],
                                in_=ap["x"][tok * 128:(tok + 1) * 128,
                                            o2 * 512:(o2 + 1) * 512]).then_inc(semh["sxa"], 16)
                    e.dma_start(out=aat[:, (t % 2) * 512:(t % 2 + 1) * 512],
                                in_=ap["attn_scr"][tok * 128:(tok + 1) * 128,
                                                   o2 * 512:(o2 + 1) * 512]).then_inc(semh["sxa"], 16)
                P.on("scalar", s_xld)
                n_x = P.inc("sxa", 32)

                def v_ev(e, t=t, idx=idx, tok=tok, o2=o2, n=n_td, nx=n_x,
                         s0=sout0, nrb=P.inc("rb", 1)):
                    e.wait_ge(semh["tb"], n)
                    e.wait_ge(semh["sxa"], nx)
                    ngo = s0 + 16 * max(idx - 1, 0)
                    if ngo > 0:
                        e.wait_ge(semh["go"], ngo)
                    e.scalar_tensor_tensor(out=qt["tm"][:, (t % 2) * 512:(t % 2 + 1) * 512],
                                           in0=bank[t][:, 0:512],
                                           scalar=PP[:, 6:7],
                                           in1=aat[:, (t % 2) * 512:(t % 2 + 1) * 512],
                                           op0=Op.mult, op1=Op.add).then_inc(semh["rb"], 1)
                    e.wait_ge(semh["rb"], nrb)
                    e.tensor_tensor(out=qt["at"][:, (t % 2) * 512:(t % 2 + 1) * 512],
                                    in0=qt["tm"][:, (t % 2) * 512:(t % 2 + 1) * 512],
                                    in1=xat[:, (t % 2) * 512:(t % 2 + 1) * 512],
                                    op=Op.add).then_inc(semh["m4"], 1)
                P.on("vector", v_ev)
                n_e = P.inc("m4", 1)

                def s_out(e, tok=tok, t=t, o2=o2, n=n_e):
                    e.wait_ge(semh["m4"], n)
                    e.dma_start(out=ap["y"][tok * 128:(tok + 1) * 128, o2 * 512:(o2 + 1) * 512],
                                in_=qt["at"][:, (t % 2) * 512:(t % 2 + 1) * 512]).then_inc(semh["go"], 16)
                P.on("scalar", s_out)
                P.inc("go", 16)

    if "dbg_pp" in ap:
        scr_b = P.val("gq")
        def g_pp(e, n=P.val("go"), sb2=scr_b):
            e.wait_ge(semh["go"], n)
            e.dma_start(out=ap["dbg_pp"][:, 0:16], in_=PP[:]).then_inc(semh["gq"], 16)
            e.dma_start(out=ap["dbg_pp"][0:1, 16:32], in_=stmp[0:1, 0:16]).then_inc(semh["gq"], 16)
            e.dma_start(out=ap["dbg_pp"][:, 32:48],
                        in_=L["gacc"][:, 0:16]).then_inc(semh["gq"], 16)
            e.wait_ge(semh["gq"], sb2 + 48)
        P.on("gpsimd", g_pp)
        P.inc("gq", 48)

    def g_fin(e, n=P.val("go"), ns=P.val("sout")):
        e.wait_ge(semh["sout"], ns)
        e.wait_ge(semh["go"], n)
    P.on("gpsimd", g_fin)


def kernel(**inputs):
    import ml_dtypes as _ml
    lay = build_layouts(inputs)
    x_all = np.asarray(inputs["L0_attn_inp"], dtype=np.float32)
    gate_w = np.asarray(inputs["L0_ffn_gate_w"], dtype=np.float32)
    up_w = np.asarray(inputs["L0_ffn_up_w"], dtype=np.float32)
    down_w = np.asarray(inputs["L0_ffn_down_w"], dtype=np.float32)
    s_in = np.asarray(inputs["L0_ffn_s_in"], dtype=np.float32).reshape(1, 1)
    s_in_down = np.asarray(inputs["L0_ffn_s_in_down"], dtype=np.float32).reshape(1, 1)
    identb = np.eye(128).astype(_ml.bfloat16)

    common = dict(gate_w=gate_w, up_w=up_w, down_w=down_w,
                  s_in=s_in, s_in_down=s_in_down, identb=identb, **lay)
    in_maps = [dict(x=np.ascontiguousarray(x_all[b]), **common) for b in range(8)]

    nc = build_core("all")
    res = run_bass_kernel_spmd(nc, in_maps, core_ids=list(range(8)))
    out = np.stack([np.asarray(res.results[b]["y"], dtype=np.float32) for b in range(8)])
    return out
